# version 12
# baseline (speedup 1.0000x reference)
import numpy as np
from contextlib import ExitStack
import concourse.bass as bass
import concourse.mybir as mybir
from concourse.bass_utils import run_bass_kernel_spmd

F32 = mybir.dt.float32
BF16 = mybir.dt.bfloat16
AF = mybir.ActivationFunctionType
ALU = mybir.AluOpType

D = 1024
NH, HD, LH = 16, 64, 8
NLS = 65
NT = 130
GT = 4
NG = 8
KVR = 6
EPS = 1e-5

DEBUG_DUMPS = False

ENGINES = ("sync", "tensor", "vector", "scalar", "gpsimd")
ESEM = {"sync": "sy", "tensor": "pe", "vector": "ve", "scalar": "sc", "gpsimd": "gp"}


class Plan:

    def __init__(self, ndmasem=24):
        self.cmds = {e: [] for e in ENGINES}
        self.cnt = {}
        self.wm = {e: {} for e in ENGINES}
        self.ndmasem = ndmasem
        self.dma_i = 0

    def val(self, sem):
        return self.cnt.get(sem, 0)

    def _bump(self, sem, amt):
        self.cnt[sem] = self.cnt.get(sem, 0) + amt
        return self.cnt[sem]

    def wait(self, eng, sem, v=None):
        if isinstance(v, list):
            for t in v:
                self.wait(eng, sem, t)
            return
        if isinstance(v, tuple):
            sem, v = v
        v = self.val(sem) if v is None else v
        if v is None or v <= 0 or v <= self.wm[eng].get(sem, 0):
            return
        self.wm[eng][sem] = v
        self.cmds[eng].append(("wait", sem, v))

    def op(self, eng, fn, inc=False, seq=True):
        self.cmds[eng].append(["op", fn, None, seq])
        if inc:
            return self.tok(eng)
        return None

    def tok(self, eng):
        sem = ESEM[eng]
        last_op = None
        trailing = False
        for c in reversed(self.cmds[eng]):
            if c[0] == "op" and isinstance(c, list):
                last_op = c
                break
            trailing = True
        if last_op is not None and last_op[2] is None:
            last_op[2] = (sem, self._bump(sem, 1))
        if last_op is not None and not trailing:
            return last_op[2][1]
        self.cmds[eng].append(["op", lambda E: E.nop(), (sem, self._bump(sem, 1))])
        return self.cmds[eng][-1][2][1]

    def dma(self, dst, src, sem="dS"):
        if sem == "st":
            v = self._bump("st", 16)
            self.cmds["sync"].append(("dma", dst, src, "st"))
            return ("st", v)
        s = f"d{self.dma_i % self.ndmasem}"
        self.dma_i += 1
        prev = self.val(s)
        if prev:
            self.wait("sync", s, prev)
        v = self._bump(s, 16)
        self.cmds["sync"].append(("dma", dst, src, s))
        return (s, v)

    def finalize_chain(self):
        for e in ("vector", "scalar", "gpsimd"):
            out = []
            prev_was_op = False
            for c in self.cmds[e]:
                if isinstance(c, list) and c[0] == "op":
                    if prev_was_op and (len(c) < 4 or c[3]):
                        out.append(("drain",))
                    prev_was_op = True
                    out.append(c)
                else:
                    out.append(c)
            self.cmds[e] = out

    def barrier(self):
        for e in ENGINES:
            for i in range(self.ndmasem):
                s = f"d{i}"
                if self.val(s):
                    self.wait(e, s, self.val(s))
        snap = {}
        for e in ENGINES:
            self.tok(e)
            snap[ESEM[e]] = self.val(ESEM[e])
        for e in ENGINES:
            for f in ENGINES:
                if f != e:
                    self.wait(e, ESEM[f], snap[ESEM[f]])


class Banks:

    def __init__(self):
        self.fresh = [True] * 8
        self.rel = [None] * 8

    def start(self, i):
        s = self.fresh[i]
        self.fresh[i] = False
        r, self.rel[i] = self.rel[i], None
        return s, r

    def renew(self, i, rel=None):
        self.fresh[i] = True
        self.rel[i] = rel

    def take(self, i):
        self.fresh[i] = False
        r, self.rel[i] = self.rel[i], None
        return r


def build_program():
    nc = bass.Bass("TRN2", target_bir_lowering=False, debug=False)

    xb_h = nc.dram_tensor("xb", [NT * 128, D], F32, kind="ExternalInput").ap()
    y_h = nc.dram_tensor("y", [4096, D], F32, kind="ExternalOutput").ap()
    dbg = {}
    if DEBUG_DUMPS:
        for nm, shp, dt in [("d_stats", [128, 4 * NT], F32),
                            ("d_pfm", [128, 8, NLS], F32),
                            ("d_gk", [128, 4, NLS], F32),
                            ("d_gv", [NLS, LH, NLS], F32),
                            ("d_misc", [1, 8], F32),
                            ("d_xnf", [128, 8, GT, 128], F32),
                            ("d_qf", [128, 8, GT, 128], F32),
                            ("d_gtm", [128, GT, 32], F32),
                            ("d_ghalf", [128, GT, 512], F32),
                            ("d_kfh", [128, 8, KVR, 128], BF16),
                            ("d_kfl", [128, 8, KVR, 128], BF16),
                            ("d_vth", [128, KVR, NH, 65], BF16),
                            ("d_vtl", [128, KVR, NH, 65], BF16)]:
            dbg[nm] = nc.dram_tensor(nm, shp, dt, kind="ExternalOutput").ap()
    win = {}
    for nm, shp in [("wq", [D, D]), ("wk", [D, D]), ("wv", [D, D]),
                    ("wgq", [D, 512]), ("wgk", [D, 512]), ("wgv", [D, 512]),
                    ("r_out", [512, D]), ("w_out", [D, D]),
                    ("cn_w1", [D, D]), ("cn_w2", [D, 512]), ("cn_w3", [512, 256]),
                    ("cn_w4", [256, 1]),
                    ("gc_w1", [D, 512]), ("gc_w2", [512, 256]), ("gc_w3", [256, 1]),
                    ("gw", [D, 32])]:
        win[nm] = nc.dram_tensor(nm, shp, F32, kind="ExternalInput").ap()
    bin_ = {}
    for nm, shp in [("cn_b1", [128, 8]), ("cn_b2", [128, 4]), ("cn_b3", [128, 2]),
                    ("cn_b4", [1, 1]), ("gc_b1", [128, 4]), ("gc_b2", [128, 2]),
                    ("gc_b3", [1, 1]), ("ar", [1, 1])]:
        bin_[nm] = nc.dram_tensor(nm, shp, F32, kind="ExternalInput").ap()
    mtri_h = nc.dram_tensor("mtri", [128, 2, 128], F32, kind="ExternalInput").ap()
    medge_h = nc.dram_tensor("medge", [128, 2], F32, kind="ExternalInput").ap()
    mlat_h = nc.dram_tensor("mlat", [NLS, 4096], F32, kind="ExternalInput").ap()
    idn_h = nc.dram_tensor("idn", [128, 128], F32, kind="ExternalInput").ap()

    scr = {}
    for nm in ["wq", "wk", "wv", "wgq", "r_out", "w_out", "cn_w1", "cn_w2", "cn_w3"]:
        R, C = win[nm].shape
        scr[nm] = nc.dram_tensor("scr_" + nm, [2, R, C], BF16, kind="Internal").ap()

    es = ExitStack()

    def sb(name, shape, dt):
        return es.enter_context(nc.sbuf_tensor(name, shape, dt))

    idn = sb("idn_s", [128, 128], F32)
    mtri = sb("mtri_s", [128, 2, 128], F32)
    medge = sb("medge_s", [128, 2], F32)
    mlat_s = sb("mlat_sl", [NLS, 512], F32)
    cnb1 = sb("cnb1", [128, 8], F32)
    cnb2 = sb("cnb2", [128, 4], F32)
    cnb3 = sb("cnb3", [128, 2], F32)
    cnb4 = sb("cnb4", [1, 1], F32)
    gcb1 = sb("gcb1", [128, 4], F32)
    gcb2 = sb("gcb2", [128, 2], F32)
    gcb3 = sb("gcb3", [1, 1], F32)
    ar_s = sb("ar_s", [1, 1], F32)
    ab32 = sb("ab32", [1, 2], F32)
    ones32 = sb("ones32", [1, 128], F32)
    gw_h = sb("gw_h", [128, 8, 32], BF16)
    gw_l = sb("gw_l", [128, 8, 32], BF16)
    cn4h = sb("cn4h", [128, 2, 1], BF16)
    cn4l = sb("cn4l", [128, 2, 1], BF16)
    ssum = sb("ssum", [128, NT], F32)
    ssq = sb("ssq", [128, NT], F32)
    mu_all = sb("mu_all", [128, NT], F32)
    r_all = sb("r_all", [128, NT], F32)
    sd_t = sb("sd_t", [128, 1], F32)
    pm32 = sb("pm32", [128, NLS], F32)
    pm_h = sb("pm_h", [128, NLS], BF16)
    pm_l = sb("pm_l", [128, NLS], BF16)
    mu_h = sb("mu_h", [128, 1], BF16)
    mu_l = sb("mu_l", [128, 1], BF16)
    corr = sb("corr", [NLS, 1], F32)
    pfm_h = sb("pfm_h", [128, 8, NLS], BF16)
    pfm_l = sb("pfm_l", [128, 8, NLS], BF16)
    gk_h = sb("gk_h", [128, 4, NLS], BF16)
    gk_l = sb("gk_l", [128, 4, NLS], BF16)
    gv_h = sb("gv_h", [NLS, LH, NLS], BF16)
    gv_l = sb("gv_l", [NLS, LH, NLS], BF16)
    gc32 = sb("gc32", [128, 8], F32)
    gc_h = sb("gc_h", [128, 8], BF16)
    gc_l = sb("gc_l", [128, 8], BF16)
    h1gc = sb("h1gc", [128, 4], F32)
    h1gc_h = sb("h1gc_h", [128, 4], BF16)
    h1gc_l = sb("h1gc_l", [128, 4], BF16)
    h2gc = sb("h2gc", [128, 2], F32)
    h2gc_h = sb("h2gc_h", [128, 2], BF16)
    h2gc_l = sb("h2gc_l", [128, 2], BF16)
    gf32 = sb("gf32", [1, 1], F32)
    xt = sb("xt", [128, 2, D], F32)
    wbh = sb("wbh", [128, 2, D], BF16)
    wbl = sb("wbl", [128, 2, D], BF16)
    arena = sb("arena", [128, 2048], F32)
    xnf_h = sb("xnf_h", [128, 8, GT, 128], BF16)
    xnf_l = sb("xnf_l", [128, 8, GT, 128], BF16)
    qf_h = sb("qf_h", [128, 8, GT, 128], BF16)
    qf_l = sb("qf_l", [128, 8, GT, 128], BF16)
    gqf_h = sb("gqf_h", [128, 4, GT, 128], BF16)
    gqf_l = sb("gqf_l", [128, 4, GT, 128], BF16)
    kf_h = sb("kf_h", [128, 8, KVR, 128], BF16)
    kf_l = sb("kf_l", [128, 8, KVR, 128], BF16)
    vt_h = sb("vt_h", [128, KVR, NH, 65], BF16)
    vt_l = sb("vt_l", [128, KVR, NH, 65], BF16)
    g_tm = sb("g_tm", [128, GT, 32], F32)
    stg = sb("stg", [128, 2, 512], F32)
    final = sb("final", [128, GT, D], F32)
    outr = sb("outr", [128, 8, GT, 128], F32)
    outh = sb("outh", [128, 8, GT, 128], BF16)
    outl = sb("outl", [128, 8, GT, 128], BF16)
    pl_h = sb("pl_h", [128, 2, 384], BF16)
    pl_l = sb("pl_l", [128, 2, 384], BF16)
    plat_h = sb("plat_h", [NLS, 512], BF16)
    plat_l = sb("plat_l", [NLS, 512], BF16)
    remf_h = sb("remf_h", [128, 4, GT, 128], BF16)
    remf_l = sb("remf_l", [128, 4, GT, 128], BF16)
    ghalf = sb("ghalf", [128, GT, 512], F32)
    gfm_h = sb("gfm_h", [128, 4, GT, 128], BF16)
    gfm_l = sb("gfm_l", [128, 4, GT, 128], BF16)
    tiny = sb("tiny", [128, 16], F32)

    gstg = stg[0:32, 1, 0:512]

    def xh_t(slot):
        return outh[:, 2 * slot:2 * slot + 2, :, :].rearrange("p a t c -> p (a t c)")

    def xl_t(slot):
        return outl[:, 2 * slot:2 * slot + 2, :, :].rearrange("p a t c -> p (a t c)")

    def wtmp(slot):
        return outr[:, 2 * slot:2 * slot + 2, :, :].rearrange("p a t c -> p (a t c)")

    def xnt(slot):
        return arena[:, 1024 * slot:1024 * (slot + 1)]

    rem_tm = ghalf
    plat32 = arena[0:NLS, 0:512]
    p32bufs = (arena[:, 512:896], arena[:, 1536:1920])
    p32_rd = {0: None, 1: None}
    tsc0 = arena[:, 0:512]
    tsc1 = arena[:, 512:1024]
    c32 = arena[:, 1024:1536]
    tf32 = arena[0:1, 1536:2048]
    c1row = arena[0:1, 0:512]

    ps = [es.enter_context(nc.psum_tensor(f"ps{i}", [128, 512], F32)) for i in range(8)]
    bk = Banks()
    sems = {}
    for s in (["st", "pe", "ve", "sc", "gp", "sy"] + [f"d{i}" for i in range(24)]):
        sems[s] = es.enter_context(nc.semaphore(s))

    P = Plan()
    stage_busy = {}

    def mm(bank_i, out_ap, lhsT, rhs, last=False):
        st_, rel = bk.start(bank_i)
        if rel is not None:
            P.wait("tensor", rel[0], rel[1])
        P.op("tensor", lambda E, o=out_ap, l=lhsT, r=rhs, s=st_, e=last:
             E.matmul(o, l, r, start=s, stop=e, skip_group_check=True))

    def mm3(bank_i, out_ap, lh, ll, rh, rl, last=False):
        mm(bank_i, out_ap, lh, rh)
        mm(bank_i, out_ap, lh, rl)
        mm(bank_i, out_ap, ll, rh, last=last)

    def dual_evac(bank_view, hi, lo, stg_slot=None, stage=None, func=None,
                  bias=None, scale=None):
        if stage is None:
            n = 1
            for d in bank_view.shape[1:]:
                n *= d
            stage = stg[:, stg_slot, 0:n]
        if stg_slot is not None and stg_slot in stage_busy:
            P.wait("scalar", "gp", stage_busy[stg_slot])
        if func is None:
            P.op("scalar", lambda E, o=stage, i=bank_view: E.copy(o, i), seq=False)
        else:
            kw = {}
            if bias is not None:
                kw["bias"] = bias
            if scale is not None:
                kw["scale"] = scale
            P.op("scalar", lambda E, o=stage, i=bank_view, f=func, k=kw:
                 E.activation(o, i, f, **k), seq=False)
        sct = P.tok("scalar")
        P.wait("vector", "sc", sct)
        P.op("vector", lambda E, o=hi, i=stage: E.tensor_copy(o, i), seq=False)
        vet = P.tok("vector")
        P.wait("gpsimd", "ve", vet)
        P.op("gpsimd", lambda E, o=lo, a=stage, b=hi: E.tensor_sub(o, a, b), seq=False)
        gpt = P.tok("gpsimd")
        if stg_slot is not None:
            stage_busy[stg_slot] = gpt
        return vet, gpt, sct

    def pe_guard():
        P.wait("tensor", "sc")
        P.wait("tensor", "ve")
        P.wait("tensor", "gp")

    def pe_take(i):
        rel = bk.take(i)
        if rel is not None:
            P.wait("tensor", rel[0], rel[1])

    P.dma(idn[:], idn_h)
    P.dma(mtri[:], mtri_h)
    P.dma(medge[:], medge_h)
    P.dma(cnb1[:], bin_["cn_b1"])
    P.dma(cnb2[:], bin_["cn_b2"])
    P.dma(cnb3[:], bin_["cn_b3"])
    P.dma(cnb4[:], bin_["cn_b4"])
    P.dma(gcb1[:], bin_["gc_b1"])
    P.dma(gcb2[:], bin_["gc_b2"])
    P.dma(gcb3[:], bin_["gc_b3"])
    P.dma(ar_s[:], bin_["ar"])
    P.op("vector", lambda E: E.memset(ones32[:], 1.0))
    for sl in range(KVR):
        P.op("vector", lambda E, s=sl: E.memset(vt_h[:, s, :, 64:65], 1.0))
        P.op("vector", lambda E, s=sl: E.memset(vt_l[:, s, :, 64:65], 0.0))
    P.op("vector", lambda E: E.memset(gv_h[:, :, 64:65], 1.0))
    P.op("vector", lambda E: E.memset(gv_l[:, :, 64:65], 0.0))
    P.op("vector", lambda E: E.memset(ssum[:], 0.0))
    P.op("vector", lambda E: E.memset(ssq[:], 0.0))
    P.tok("vector")

    slot_state = {}

    def split_chunk(src_ap, rows, C, slot, hi_dst, lo_dst, to_scratch):
        if slot in slot_state:
            vt0, gt0, dt0 = slot_state[slot]
            P.wait("sync", "ve", vt0)
            P.wait("sync", "gp", gt0)
            if dt0:
                P.wait("vector", "dS", dt0)
                P.wait("gpsimd", "dS", dt0)
                P.wait("scalar", "dS", dt0)
        t = P.dma(wtmp(slot)[0:rows, 0:C], src_ap)
        P.wait("vector", "dS", t)
        hdst = xh_t(slot)[0:rows, 0:C] if to_scratch else hi_dst
        ldst = xl_t(slot)[0:rows, 0:C] if to_scratch else lo_dst
        P.op("vector", lambda E, o=hdst, i=wtmp(slot)[0:rows, 0:C]: E.tensor_copy(o, i),
             seq=False)
        vt = P.tok("vector")
        P.wait("gpsimd", "ve", vt)
        P.op("gpsimd", lambda E, o=ldst, a=wtmp(slot)[0:rows, 0:C], b=hdst:
             E.tensor_sub(o, a, b), seq=False)
        gt_ = P.tok("gpsimd")
        dt = 0
        if to_scratch:
            P.wait("sync", "ve", vt)
            P.wait("sync", "gp", gt_)
            t1 = P.dma(hi_dst, hdst)
            t2 = P.dma(lo_dst, ldst)
            dt = [t1, t2]
        slot_state[slot] = (vt, gt_, dt)
        return vt, gt_

    ci_all = 0
    for nm in ["wq", "wk", "wv", "wgq", "r_out", "w_out", "cn_w1", "cn_w2", "cn_w3"]:
        R, C = win[nm].shape
        for ci in range((R + 127) // 128):
            rows = min(128, R - 128 * ci)
            split_chunk(win[nm][128 * ci:128 * ci + rows, :], rows, C, ci_all % 2,
                        scr[nm][0, 128 * ci:128 * ci + rows, :],
                        scr[nm][1, 128 * ci:128 * ci + rows, :], True)
            ci_all += 1
    for ci in range(8):
        split_chunk(win["gw"][128 * ci:128 * (ci + 1), :], 128, 32, ci_all % 2,
                    gw_h[:, ci, :], gw_l[:, ci, :], False)
        ci_all += 1
    for ci in range(2):
        split_chunk(win["cn_w4"][128 * ci:128 * (ci + 1), :], 128, 1, ci_all % 2,
                    cn4h[:, ci, :], cn4l[:, ci, :], False)
        ci_all += 1
    P.barrier()

    pe_tile = {}
    for t in range(NT):
        slot = t % 2
        if t >= 2:
            P.wait("sync", "pe", pe_tile[t - 2])
            P.wait("scalar", "pe", pe_tile[t - 2])
            P.wait("gpsimd", "pe", pe_tile[t - 2])
        xtok = P.dma(xt[:, slot, :], xb_h[128 * t:128 * (t + 1), :])
        P.wait("scalar", "dS", xtok)
        P.op("scalar", lambda E, s=slot, tt=t: E.activation(
            xh_t(s), xt[:, s, :], AF.Identity, accum_out=ssum[:, tt:tt + 1]), seq=False)
        P.op("scalar", lambda E, s=slot, tt=t: E.activation(
            xl_t(s), xt[:, s, :], AF.Square, accum_out=ssq[:, tt:tt + 1]), seq=False)
        sct = P.tok("scalar")
        P.wait("gpsimd", "sc", sct)
        P.op("gpsimd", lambda E, s=slot: E.tensor_sub(
            xl_t(s), xt[:, s, :], xh_t(s)), seq=False)
        gpt = P.tok("gpsimd")
        P.wait("vector", "sc", sct)
        P.op("vector", lambda E, tt=t: E.tensor_scalar(
            mu_all[:, tt:tt + 1], ssum[:, tt:tt + 1], 1.0 / 1024, None, ALU.mult))
        P.op("vector", lambda E, tt=t: E.tensor_scalar(
            sd_t[:], ssq[:, tt:tt + 1], 1.0 / 1024, EPS, ALU.mult, ALU.add))
        P.op("vector", lambda E, tt=t: E.tensor_tensor(
            tiny[:, 0:1], mu_all[:, tt:tt + 1], mu_all[:, tt:tt + 1], ALU.mult))
        P.op("vector", lambda E: E.tensor_sub(sd_t[:], sd_t[:], tiny[:, 0:1]))
        v0 = P.tok("vector")
        P.wait("scalar", "ve", v0)
        P.op("scalar", lambda E: E.activation(sd_t[:], sd_t[:], AF.Sqrt), seq=False)
        sct2 = P.tok("scalar")
        P.wait("vector", "sc", sct2)
        P.op("vector", lambda E, tt=t: E.reciprocal(r_all[:, tt:tt + 1], sd_t[:]))
        ch = t // 2
        if t >= 1:
            P.wait("vector", "pe", pe_tile[t - 1])
            P.wait("gpsimd", "pe", pe_tile[t - 1])
        if t == 0:
            P.op("vector", lambda E: E.memset(pm32[:], 0.0))
        elif t % 2 == 0:
            P.op("vector", lambda E, c=ch - 1: E.memset(pm32[:, c:c + 1], 0.0))
        P.op("vector", lambda E, tt=t, c=ch: E.tensor_scalar(
            pm32[:, c:c + 1], r_all[:, tt:tt + 1], 1.0 / 256, None, ALU.mult))
        P.op("vector", lambda E: E.tensor_copy(pm_h[:], pm32[:]))
        P.op("vector", lambda E, tt=t: E.tensor_copy(mu_h[:], mu_all[:, tt:tt + 1]))
        vt = P.tok("vector")
        P.wait("gpsimd", "ve", vt)
        P.op("gpsimd", lambda E: E.tensor_sub(pm_l[:], pm32[:], pm_h[:]), seq=False)
        P.op("gpsimd", lambda E, tt=t: E.tensor_sub(mu_l[:], mu_all[:, tt:tt + 1], mu_h[:]),
             seq=False)
        gpt2 = P.tok("gpsimd")
        P.wait("tensor", "ve", vt)
        P.wait("tensor", "gp", gpt2)
        last = (t == NT - 1)
        mm3(0, ps[0][0:NLS, :], pm_h[:], pm_l[:],
            xh_t(slot)[:, 0:512], xl_t(slot)[:, 0:512], last=last)
        mm3(1, ps[1][0:NLS, :], pm_h[:], pm_l[:],
            xh_t(slot)[:, 512:1024], xl_t(slot)[:, 512:1024], last=last)
        mm3(2, ps[2][0:NLS, 0:1], pm_h[:], pm_l[:], mu_h[:], mu_l[:], last=last)
        pe_tile[t] = P.tok("tensor")
    P.barrier()

    pooled = final[0:NLS, 0, :]
    P.op("vector", lambda E: E.tensor_copy(corr[:], ps[2][0:NLS, 0:1]))
    P.op("vector", lambda E: E.tensor_scalar(
        pooled[:, 0:512], ps[0][0:NLS, :], corr[:], None, ALU.subtract))
    P.op("vector", lambda E: E.tensor_scalar(
        pooled[:, 512:1024], ps[1][0:NLS, :], corr[:], None, ALU.subtract))
    ptok = P.tok("vector")
    for i in range(3):
        bk.renew(i)
    P.wait("tensor", "ve", ptok)
    for f in range(8):
        bi = 3 + (f % 2)
        pe_take(bi)
        P.op("tensor", lambda E, ff=f, b=bi: E.transpose(
            ps[b][:, 0:NLS], pooled[:, 128 * ff:128 * (ff + 1)], idn[0:NLS, 0:NLS]))
        ttok = P.tok("tensor")
        P.wait("scalar", "pe", ttok)
        _, _, sct = dual_evac(ps[bi][:, 0:NLS], pfm_h[:, f, :], pfm_l[:, f, :],
                              stg_slot=f % 2)
        bk.renew(bi, ("sc", sct))
    P.barrier()
    for f in range(8):
        P.op("vector", lambda E, ff=f: E.tensor_reduce(
            tiny[:, 0:1], pfm_h[:, ff, :], axis=mybir.AxisListType.X, op=ALU.add))
        P.op("vector", lambda E, ff=f: E.tensor_reduce(
            tiny[:, 1:2], pfm_l[:, ff, :], axis=mybir.AxisListType.X, op=ALU.add))
        P.op("vector", lambda E: E.tensor_add(tiny[:, 0:1], tiny[:, 0:1], tiny[:, 1:2]))
        P.op("vector", lambda E, ff=f: E.tensor_scalar(
            gc32[:, ff:ff + 1], tiny[:, 0:1], 1.0 / 64, None, ALU.mult))
    P.op("vector", lambda E: E.tensor_copy(gc_h[:], gc32[:]))
    vt = P.tok("vector")
    P.wait("gpsimd", "ve", vt)
    P.op("gpsimd", lambda E: E.tensor_sub(gc_l[:], gc32[:], gc_h[:]))
    P.tok("gpsimd")
    P.barrier()

    wslot = {"i": 0, "tok": {}}

    def wstream(src_hi, src_lo=None, C=512):
        sl = wslot["i"] % 2
        wslot["i"] += 1
        if sl in wslot["tok"]:
            P.wait("sync", "pe", wslot["tok"][sl])
        if src_lo is None:
            vt_, gt_ = split_chunk(src_hi, 128, C, sl, wbh[:, sl, 0:C],
                                   wbl[:, sl, 0:C], False)
            P.wait("tensor", "ve", vt_)
            P.wait("tensor", "gp", gt_)
            return sl, None
        t1 = P.dma(wbh[:, sl, 0:C], src_hi)
        t2 = P.dma(wbl[:, sl, 0:C], src_lo)
        P.wait("tensor", "dS", [t1, t2])
        return sl, [t1, t2]

    def wmark(sl):
        wslot["tok"][sl] = P.tok("tensor")

    for f in range(8):
        sl, _ = wstream(win["wgk"][128 * f:128 * (f + 1), :], None, 512)
        for o in range(4):
            mm3(5, ps[5][:, 66 * o:66 * o + NLS],
                wbh[:, sl, 128 * o:128 * (o + 1)], wbl[:, sl, 128 * o:128 * (o + 1)],
                pfm_h[:, f, :], pfm_l[:, f, :], last=(f == 7))
        wmark(sl)
    tk = P.val("pe")
    P.wait("scalar", "pe", tk)
    dual_evac(ps[5][:, 0:264].rearrange("p (o c) -> p o c", c=66)[:, :, 0:NLS],
              gk_h[:], gk_l[:], stg_slot=0,
              stage=stg[:, 0, 0:264].rearrange("p (o c) -> p o c", c=66)[:, :, 0:NLS])
    bk.renew(5)
    for f in range(8):
        sl, _ = wstream(win["wgv"][128 * f:128 * (f + 1), :], None, 512)
        mm3(6, ps[6][0:NLS, :], pfm_h[:, f, :], pfm_l[:, f, :],
            wbh[:, sl, 0:512], wbl[:, sl, 0:512], last=(f == 7))
        wmark(sl)
    tk = P.val("pe")
    P.wait("scalar", "pe", tk)
    dual_evac(ps[6][0:NLS, 0:512].rearrange("p (h c) -> p h c", c=64),
              gv_h[:, :, 0:64], gv_l[:, :, 0:64], stg_slot=1,
              stage=stg[0:NLS, 1, 0:512].rearrange("p (h c) -> p h c", c=64))
    bk.renew(6)
    P.barrier()
    for f in range(8):
        sl, _ = wstream(win["gc_w1"][128 * f:128 * (f + 1), :], None, 512)
        for o in range(4):
            mm3(7, ps[7][:, 2 * o:2 * o + 1],
                wbh[:, sl, 128 * o:128 * (o + 1)], wbl[:, sl, 128 * o:128 * (o + 1)],
                gc_h[:, f:f + 1], gc_l[:, f:f + 1], last=(f == 7))
        wmark(sl)
    P.wait("scalar", "pe")
    for o in range(4):
        P.op("scalar", lambda E, oo=o: E.activation(
            h1gc[:, oo:oo + 1], ps[7][:, 2 * oo:2 * oo + 1], AF.Gelu,
            bias=gcb1[:, oo:oo + 1]))
    sct = P.tok("scalar")
    bk.renew(7)
    P.wait("vector", "sc", sct)
    P.op("vector", lambda E: E.tensor_copy(h1gc_h[:], h1gc[:]))
    vt = P.tok("vector")
    P.wait("gpsimd", "ve", vt)
    P.op("gpsimd", lambda E: E.tensor_sub(h1gc_l[:], h1gc[:], h1gc_h[:]))
    gt2 = P.tok("gpsimd")
    P.wait("tensor", "ve", vt)
    P.wait("tensor", "gp", gt2)
    P.wait("tensor", "sc", sct)
    for f in range(4):
        sl, _ = wstream(win["gc_w2"][128 * f:128 * (f + 1), :], None, 256)
        for o in range(2):
            mm3(7, ps[7][:, 2 * o:2 * o + 1],
                wbh[:, sl, 128 * o:128 * (o + 1)], wbl[:, sl, 128 * o:128 * (o + 1)],
                h1gc_h[:, f:f + 1], h1gc_l[:, f:f + 1], last=(f == 3))
        wmark(sl)
    P.wait("scalar", "pe")
    for o in range(2):
        P.op("scalar", lambda E, oo=o: E.activation(
            h2gc[:, oo:oo + 1], ps[7][:, 2 * oo:2 * oo + 1], AF.Gelu,
            bias=gcb2[:, oo:oo + 1]))
    sct = P.tok("scalar")
    bk.renew(7)
    P.wait("vector", "sc", sct)
    P.op("vector", lambda E: E.tensor_copy(h2gc_h[:], h2gc[:]))
    vt = P.tok("vector")
    P.wait("gpsimd", "ve", vt)
    P.op("gpsimd", lambda E: E.tensor_sub(h2gc_l[:], h2gc[:], h2gc_h[:]))
    gt2 = P.tok("gpsimd")
    P.wait("tensor", "ve", vt)
    P.wait("tensor", "gp", gt2)
    P.wait("tensor", "sc", sct)
    for f in range(2):
        sl, _ = wstream(win["gc_w3"][128 * f:128 * (f + 1), :], None, 1)
        mm3(7, ps[7][0:1, 0:1], wbh[:, sl, 0:1], wbl[:, sl, 0:1],
            h2gc_h[:, f:f + 1], h2gc_l[:, f:f + 1], last=(f == 1))
        wmark(sl)
    P.wait("scalar", "pe")
    P.op("scalar", lambda E: E.activation(gf32[:], ps[7][0:1, 0:1], AF.Sigmoid,
                                          bias=gcb3[:]))
    sct = P.tok("scalar")
    bk.renew(7)
    P.wait("vector", "sc", sct)
    P.op("vector", lambda E: E.tensor_scalar(
        tiny[0:1, 0:1], gf32[:], 0.7, -0.5, ALU.mult, ALU.add))
    P.op("vector", lambda E: E.tensor_tensor(tiny[0:1, 0:1], tiny[0:1, 0:1],
                                             ar_s[:], ALU.mult))
    P.op("vector", lambda E: E.tensor_scalar(
        ab32[0:1, 0:1], tiny[0:1, 0:1], 2.0, 0.1, ALU.mult, ALU.add))
    P.op("vector", lambda E: E.tensor_scalar(
        ab32[0:1, 1:2], ar_s[:], 0.6, None, ALU.mult))
    P.tok("vector")
    P.barrier()

    def flat(ap):
        return ap.rearrange("p t c -> p (t c)")

    xslot_tok = {}
    xnslot_tok = {}

    def proj_tiles(perm_tiles, is_group, g):
        ntl = len(perm_tiles)
        kvpos = [pt + 2 if pt < 32 else pt - 32 for pt in perm_tiles]
        evk = {}
        for ii, pt in enumerate(perm_tiles):
            slot = ii % 2
            if slot in xslot_tok:
                P.wait("sync", "ve", xslot_tok[slot])
            xtok = P.dma(xt[:, slot, :], xb_h[128 * pt:128 * (pt + 1), :])
            P.wait("vector", "dS", xtok)
            if slot in xnslot_tok:
                P.wait("vector", "pe", xnslot_tok[slot])
            P.op("vector", lambda E, s=slot, t=pt: E.tensor_scalar(
                xnt(s), xt[:, s, :], mu_all[:, t:t + 1], r_all[:, t:t + 1],
                ALU.subtract, ALU.mult))
            vt = P.tok("vector")
            xslot_tok[slot] = vt
            P.wait("tensor", "ve", vt)
            for half in range(2):
                b = half
                pe_take(b)
                for fi in range(4):
                    f = 4 * half + fi
                    P.op("tensor", lambda E, s=slot, ff=f, bb=b, fj=fi: E.transpose(
                        ps[bb][:, 128 * fj:128 * (fj + 1)],
                        xnt(s)[:, 128 * ff:128 * (ff + 1)], idn[:]))
                ttok = P.tok("tensor")
                if half == 1:
                    xnslot_tok[slot] = ttok
                P.wait("scalar", "pe", ttok)
                bv = ps[b][:, 0:512].rearrange("p (f c) -> p f c", c=128)
                if is_group:
                    stage = outr[:, 4 * half:4 * half + 4, ii, :]
                    slot_arg = None
                else:
                    stage = stg[:, half, 0:512].rearrange("p (f c) -> p f c", c=128)
                    slot_arg = half
                ve_t, gp_t, sct = dual_evac(
                    bv, xnf_h[:, 4 * half:4 * half + 4, ii, :],
                    xnf_l[:, 4 * half:4 * half + 4, ii, :], stg_slot=slot_arg,
                    stage=stage)
                if is_group:
                    evk.setdefault(half, []).append((ve_t, gp_t))
                else:
                    P.wait("tensor", "ve", ve_t)
                    P.wait("tensor", "gp", gp_t)
                bk.renew(b, ("sc", sct))
        ntok = 128 * ntl

        for f in range(8):
            sl, _ = wstream(scr["wk"][0, 128 * f:128 * (f + 1), :],
                            scr["wk"][1, 128 * f:128 * (f + 1), :], 1024)
            if f % 4 == 0 and (f // 4) in evk:
                for ve_t, gp_t in evk.pop(f // 4):
                    P.wait("tensor", "ve", ve_t)
                    P.wait("tensor", "gp", gp_t)
            for o in range(8):
                mm3(o, ps[o][:, 0:ntok],
                    wbh[:, sl, 128 * o:128 * (o + 1)], wbl[:, sl, 128 * o:128 * (o + 1)],
                    flat(xnf_h[:, f, 0:ntl, :]), flat(xnf_l[:, f, 0:ntl, :]),
                    last=(f == 7))
            wmark(sl)
        P.wait("scalar", "pe")
        kv_done = []
        for o in range(8):
            for ii, j in enumerate(kvpos):
                sl6 = j % KVR
                _, _, sct = dual_evac(ps[o][:, 128 * ii:128 * (ii + 1)],
                                      kf_h[:, o, sl6, :], kf_l[:, o, sl6, :],
                                      stg_slot=o % 2)
            bk.renew(o, ("sc", sct))
        for f in range(8):
            sl, _ = wstream(scr["wv"][0, 128 * f:128 * (f + 1), :],
                            scr["wv"][1, 128 * f:128 * (f + 1), :], 1024)
            for ii in range(ntl):
                for oh in range(2):
                    bi = 2 * ii + oh
                    mm3(bi, ps[bi][:, 0:512],
                        xnf_h[:, f, ii, :], xnf_l[:, f, ii, :],
                        wbh[:, sl, 512 * oh:512 * (oh + 1)],
                        wbl[:, sl, 512 * oh:512 * (oh + 1)], last=(f == 7))
            wmark(sl)
        P.wait("scalar", "pe")
        for ii, j in enumerate(kvpos):
            sl6 = j % KVR
            for oh in range(2):
                bi = 2 * ii + oh
                _, _, sct = dual_evac(
                    ps[bi][:, 0:512].rearrange("p (h c) -> p h c", c=64),
                    vt_h[:, sl6, 8 * oh:8 * oh + 8, 0:64],
                    vt_l[:, sl6, 8 * oh:8 * oh + 8, 0:64], stg_slot=oh,
                    stage=stg[:, oh, 0:512].rearrange("p (h c) -> p h c", c=64))
                bk.renew(bi, ("sc", sct))
        if not is_group:
            return

        for f in range(8):
            sl, _ = wstream(scr["wq"][0, 128 * f:128 * (f + 1), :],
                            scr["wq"][1, 128 * f:128 * (f + 1), :], 1024)
            for o in range(8):
                mm3(o, ps[o][:, 0:ntok],
                    wbh[:, sl, 128 * o:128 * (o + 1)], wbl[:, sl, 128 * o:128 * (o + 1)],
                    flat(xnf_h[:, f, 0:ntl, :]), flat(xnf_l[:, f, 0:ntl, :]),
                    last=(f == 7))
            wmark(sl)
        P.wait("scalar", "pe")
        q_ve = q_gp = None
        for o in range(8):
            q_ve, q_gp, sct = dual_evac(ps[o][:, 0:ntok], flat(qf_h[:, o, 0:ntl, :]),
                                        flat(qf_l[:, o, 0:ntl, :]), stg_slot=o % 2)
            bk.renew(o, ("sc", sct))
        P.wait("tensor", "ve", q_ve)
        P.wait("tensor", "gp", q_gp)
        for f in range(8):
            mm3(0, ps[0][0:32, 0:ntok], gw_h[:, f, :], gw_l[:, f, :],
                flat(qf_h[:, f, 0:ntl, :]), flat(qf_l[:, f, 0:ntl, :]), last=(f == 7))
        gmtk = P.tok("tensor")
        P.wait("scalar", "pe", gmtk)
        if 1 in stage_busy:
            P.wait("scalar", "gp", stage_busy[1])
        P.op("scalar", lambda E, n=ntok: E.copy(gstg[:, 0:n], ps[0][0:32, 0:n]))
        sct = P.tok("scalar")
        bk.renew(0, ("sc", sct))
        P.wait("tensor", "sc", sct)
        for ii in range(ntl):
            P.op("tensor", lambda E, i2=ii: E.transpose(
                ps[1][:, 32 * i2:32 * (i2 + 1)], gstg[0:32, 128 * i2:128 * (i2 + 1)],
                idn[0:32, 0:32]))
        tk2 = P.tok("tensor")
        P.wait("scalar", "pe", tk2)
        P.op("scalar", lambda E, n=ntl: E.activation(
            flat(g_tm[:, 0:n, :]), ps[1][:, 0:32 * n], AF.Sigmoid))
        sct = P.tok("scalar")
        bk.renew(1, ("sc", sct))
        for f in range(8):
            sl, _ = wstream(scr["wgq"][0, 128 * f:128 * (f + 1), :],
                            scr["wgq"][1, 128 * f:128 * (f + 1), :], 512)
            for o in range(4):
                mm3(2 + o, ps[2 + o][:, 0:ntok],
                    wbh[:, sl, 128 * o:128 * (o + 1)], wbl[:, sl, 128 * o:128 * (o + 1)],
                    flat(xnf_h[:, f, 0:ntl, :]), flat(xnf_l[:, f, 0:ntl, :]),
                    last=(f == 7))
            wmark(sl)
        P.wait("scalar", "pe")
        for o in range(4):
            _, _, sct = dual_evac(ps[2 + o][:, 0:ntok], flat(gqf_h[:, o, 0:ntl, :]),
                                  flat(gqf_l[:, o, 0:ntl, :]), stg_slot=o % 2)
            bk.renew(2 + o, ("sc", sct))

    def attention(g):
        T0 = GT * g
        mtok = P.dma(mlat_s[:], mlat_h[:, 512 * g:512 * (g + 1)])
        for h8 in range(LH):
            bsc = h8 % 2
            ch, ro = h8 // 2, 64 * (h8 % 2)
            mm3(bsc, ps[bsc][0:NLS, 0:512],
                gk_h[ro:ro + 64, ch, :], gk_l[ro:ro + 64, ch, :],
                flat(gqf_h[ro:ro + 64, ch, :, :]), flat(gqf_l[ro:ro + 64, ch, :, :]),
                last=True)
            tk = P.tok("tensor")
            P.wait("scalar", "pe", tk)
            P.op("scalar", lambda E, b=bsc: E.activation(
                plat32, ps[b][0:NLS, 0:512], AF.Exp, scale=0.125), seq=False)
            sct = P.tok("scalar")
            bk.renew(bsc, ("sc", sct))
            P.wait("gpsimd", "sc", sct)
            P.wait("gpsimd", "dS", mtok)
            P.op("gpsimd", lambda E: E.tensor_tensor(plat32, plat32, mlat_s[:],
                                                     ALU.mult), seq=False)
            g1 = P.tok("gpsimd")
            P.wait("vector", "gp", g1)
            P.op("vector", lambda E: E.tensor_copy(plat_h[:], plat32), seq=False)
            v1 = P.tok("vector")
            P.wait("gpsimd", "ve", v1)
            P.op("gpsimd", lambda E: E.tensor_sub(plat_l[:], plat32, plat_h[:]), seq=False)
            g2 = P.tok("gpsimd")
            P.wait("tensor", "ve", v1)
            P.wait("tensor", "gp", g2)
            for ti in range(GT):
                bpv = 2 + (ti % 2)
                mm3(bpv, ps[bpv][:, 0:65],
                    plat_h[:, 128 * ti:128 * (ti + 1)], plat_l[:, 128 * ti:128 * (ti + 1)],
                    gv_h[:, h8, :], gv_l[:, h8, :], last=True)
                tk = P.tok("tensor")
                P.wait("vector", "pe", tk)
                P.op("vector", lambda E, b=bpv: E.reciprocal(tiny[:, 0:1],
                                                             ps[b][:, 64:65]), seq=False)
                P.op("vector", lambda E, b=bpv, t=ti, h=h8: E.tensor_scalar(
                    rem_tm[:, t, 64 * h:64 * (h + 1)], ps[b][:, 0:64], tiny[:, 0:1],
                    None, ALU.mult))
                vtk = P.tok("vector")
                bk.renew(bpv, ("ve", vtk))
        for ti in range(GT):
            pe_take(4)
            for fi in range(4):
                P.op("tensor", lambda E, t=ti, f=fi: E.transpose(
                    ps[4][:, 128 * f:128 * (f + 1)], rem_tm[:, t, 128 * f:128 * (f + 1)],
                    idn[:]))
            tk = P.tok("tensor")
            P.wait("scalar", "pe", tk)
            ve_t, gp_t, sct4 = dual_evac(
                ps[4][:, 0:512].rearrange("p (f c) -> p f c", c=128),
                remf_h[:, :, ti, :], remf_l[:, :, ti, :], stg_slot=ti % 2,
                stage=stg[:, ti % 2, 0:512].rearrange("p (f c) -> p f c", c=128))
            P.wait("tensor", "ve", ve_t)
            P.wait("tensor", "gp", gp_t)
            bk.renew(4, ("sc", sct4))
        P.barrier()

        for sw in range(2):
            for f in range(4):
                sl = wslot["i"] % 2
                wslot["i"] += 1
                if sl in wslot["tok"]:
                    P.wait("sync", "pe", wslot["tok"][sl])
                t1 = P.dma(wbh[:, sl, 0:512],
                           scr["r_out"][0, 128 * f:128 * (f + 1), 512 * sw:512 * (sw + 1)])
                t2 = P.dma(wbl[:, sl, 0:512],
                           scr["r_out"][1, 128 * f:128 * (f + 1), 512 * sw:512 * (sw + 1)])
                P.wait("tensor", "dS", [t1, t2])
                for ti in range(GT):
                    mm3(ti, ps[ti][:, 0:512],
                        remf_h[:, f, ti, :], remf_l[:, f, ti, :],
                        wbh[:, sl, 0:512], wbl[:, sl, 0:512], last=(f == 3))
                wmark(sl)
            P.wait("vector", "pe")
            for ti in range(GT):
                for hh in range(8):
                    ha = 8 * sw + hh
                    P.op("vector", lambda E, t=ti, h=hh, a=ha: E.tensor_scalar(
                        ghalf[:, t, 64 * h:64 * (h + 1)], ps[t][:, 64 * h:64 * (h + 1)],
                        g_tm[:, t, 16 + a:17 + a], None, ALU.mult), seq=False)
            vtk = P.tok("vector")
            for ti in range(GT):
                bk.renew(ti, ("ve", vtk))

            comp_rel = {}

            def pv_reg(T, hh):
                base = (0, 2, 4, 0)[T - T0]
                return base + hh // 4, 66 * (hh % 4)

            stages = []
            for kj in range(6):
                kvj = 4 * g + kj
                Tlo = max(T0, kvj - 2)
                Thi = min(T0 + GT - 1, kvj)
                for hh, ha in enumerate(range(8 * sw, 8 * sw + 8)):
                    stages.append((kj, kvj, Tlo, Thi, hh, ha))

            def issue_front(si):
                kj, kvj, Tlo, Thi, hh, ha = stages[si]
                w = Thi - Tlo + 1
                ch, ro = ha // 2, 64 * (ha % 2)
                bsc = 6 + (si % 2)
                sl6 = kvj % KVR
                pb = si % 2
                pp = p32bufs[pb]
                mm3(bsc, ps[bsc][:, 0:128 * w],
                    kf_h[ro:ro + 64, ch, sl6, :], kf_l[ro:ro + 64, ch, sl6, :],
                    flat(qf_h[ro:ro + 64, ch, Tlo - T0:Thi - T0 + 1, :]),
                    flat(qf_l[ro:ro + 64, ch, Tlo - T0:Thi - T0 + 1, :]), last=True)
                tk = P.tok("tensor")
                P.wait("scalar", "pe", tk)
                if p32_rd[pb] is not None:
                    P.wait("scalar", "ve", p32_rd[pb])
                P.op("scalar", lambda E, b=bsc, ww=w, pp=pp: E.activation(
                    pp[:, 0:128 * ww], ps[b][:, 0:128 * ww], AF.Exp, scale=0.125),
                    seq=False)
                sct = P.tok("scalar")
                bk.renew(bsc, ("sc", sct))
                P.wait("gpsimd", "sc", sct)
                nm = 0
                for bi_, T in enumerate(range(Tlo, Thi + 1)):
                    off = T - (kvj - 2)
                    if off == 0:
                        P.op("gpsimd", lambda E, b=bi_, pp=pp: E.tensor_tensor(
                            pp[:, 128 * b:128 * (b + 1)],
                            pp[:, 128 * b:128 * (b + 1)], mtri[:, 0, :], ALU.mult),
                            seq=False)
                        nm += 1
                    elif off == 2:
                        P.op("gpsimd", lambda E, b=bi_, pp=pp: E.tensor_tensor(
                            pp[:, 128 * b:128 * (b + 1)],
                            pp[:, 128 * b:128 * (b + 1)], mtri[:, 1, :], ALU.mult),
                            seq=False)
                        nm += 1
                if kvj < 2:
                    if nm:
                        gmt = P.tok("gpsimd")
                        P.wait("gpsimd", "gp", gmt)
                    P.op("gpsimd", lambda E, j=kvj, ww=w, pp=pp: E.tensor_scalar(
                        pp[:, 0:128 * ww], pp[:, 0:128 * ww],
                        medge[:, j:j + 1], None, ALU.mult), seq=False)
                g1 = P.tok("gpsimd")
                P.wait("vector", "gp", g1)
                P.op("vector", lambda E, ww=w, b=pb, pp=pp: E.tensor_copy(
                    pl_h[:, b, 0:128 * ww], pp[:, 0:128 * ww]), seq=False)
                P.op("vector", lambda E, ww=w, b=pb, pp=pp: E.tensor_sub(
                    pl_l[:, b, 0:128 * ww], pp[:, 0:128 * ww], pl_h[:, b, 0:128 * ww]))
                v1 = P.tok("vector")
                p32_rd[pb] = v1
                return v1

            def issue_pv(si, v1):
                kj, kvj, Tlo, Thi, hh, ha = stages[si]
                sl6 = kvj % KVR
                pb = si % 2
                if kj == 3 and hh == 0:
                    bk.renew(0, comp_rel.get(T0))
                    bk.renew(1, comp_rel.get(T0))
                P.wait("tensor", "ve", v1)
                for bi_, T in enumerate(range(Tlo, Thi + 1)):
                    bb, cc = pv_reg(T, hh)
                    mm3(bb, ps[bb][:, cc:cc + 65],
                        pl_h[:, pb, 128 * bi_:128 * (bi_ + 1)],
                        pl_l[:, pb, 128 * bi_:128 * (bi_ + 1)],
                        vt_h[:, sl6, ha, :], vt_l[:, sl6, ha, :],
                        last=(kvj == T + 2))
                if hh == 7:
                    Tc = kvj - 2
                    if Tc >= T0:
                        tk = P.tok("tensor")
                        P.wait("vector", "pe", tk)
                        tl = Tc - T0
                        for h2, a2 in enumerate(range(8 * sw, 8 * sw + 8)):
                            bb, cc = pv_reg(Tc, h2)
                            P.op("vector", lambda E, b=bb, c=cc, h=h2: E.reciprocal(
                                tiny[:, h:h + 1], ps[b][:, c + 64:c + 65]), seq=False)
                            P.op("vector", lambda E, h=h2, t=tl, a=a2: E.tensor_tensor(
                                tiny[:, h:h + 1], tiny[:, h:h + 1],
                                g_tm[:, t, a:a + 1], ALU.mult))
                            P.op("vector", lambda E, b=bb, c=cc, h=h2, t=tl:
                                 E.scalar_tensor_tensor(
                                     ghalf[:, t, 64 * h:64 * (h + 1)], ps[b][:, c:c + 64],
                                     tiny[:, h:h + 1], ghalf[:, t, 64 * h:64 * (h + 1)],
                                     ALU.mult, ALU.add))
                        vtk = P.tok("vector")
                        comp_rel[Tc] = ("ve", vtk)

            prev_v1 = None
            for si in range(len(stages)):
                v1 = issue_front(si)
                if si > 0:
                    issue_pv(si - 1, prev_v1)
                prev_v1 = v1
            issue_pv(len(stages) - 1, prev_v1)
            for b in range(6):
                bk.renew(b, comp_rel.get(T0 + 3))
            for ti in range(GT):
                pe_take(6)
                for fi in range(4):
                    P.op("tensor", lambda E, t=ti, f=fi: E.transpose(
                        ps[6][:, 128 * f:128 * (f + 1)],
                        ghalf[:, t, 128 * f:128 * (f + 1)], idn[:]))
                tk = P.tok("tensor")
                P.wait("scalar", "pe", tk)
                ve_t, gp_t, sct6 = dual_evac(
                    ps[6][:, 0:512].rearrange("p (f c) -> p f c", c=128),
                    gfm_h[:, :, ti, :], gfm_l[:, :, ti, :], stg_slot=ti % 2,
                    stage=stg[:, ti % 2, 0:512].rearrange("p (f c) -> p f c", c=128))
                P.wait("tensor", "ve", ve_t)
                P.wait("tensor", "gp", gp_t)
                bk.renew(6, ("sc", sct6))
            for f in range(4):
                sl = wslot["i"] % 2
                wslot["i"] += 1
                if sl in wslot["tok"]:
                    P.wait("sync", "pe", wslot["tok"][sl])
                t1 = P.dma(wbh[:, sl, :],
                           scr["w_out"][0, 512 * sw + 128 * f:512 * sw + 128 * (f + 1), :])
                t2 = P.dma(wbl[:, sl, :],
                           scr["w_out"][1, 512 * sw + 128 * f:512 * sw + 128 * (f + 1), :])
                P.wait("tensor", "dS", [t1, t2])
                for o in range(8):
                    mm3(o, ps[o][:, 0:512],
                        wbh[:, sl, 128 * o:128 * (o + 1)], wbl[:, sl, 128 * o:128 * (o + 1)],
                        flat(gfm_h[:, f, :, :]), flat(gfm_l[:, f, :, :]), last=(f == 3))
                wmark(sl)
            for o in range(8):
                P.wait("scalar", "pe")
                if (o % 2) in stage_busy:
                    P.wait("scalar", "gp", stage_busy[o % 2])
                P.op("scalar", lambda E, b=o: E.copy(stg[:, b % 2, :], ps[b][:, 0:512]))
                sct = P.tok("scalar")
                P.wait("tensor", "sc", sct)
                for ti in range(GT):
                    P.op("tensor", lambda E, b=o, t=ti: E.transpose(
                        ps[b][:, 128 * t:128 * (t + 1)],
                        stg[:, b % 2, 128 * t:128 * (t + 1)], idn[:]))
                tk = P.tok("tensor")
                stage_busy[o % 2] = 0
                P.wait("vector", "pe", tk)
                dst = final[:, :, 128 * o:128 * (o + 1)]
                src6 = ps[o][:, 0:512].rearrange("p (t c) -> p t c", c=128)
                if sw == 0:
                    P.op("vector", lambda E, d=dst, s6=src6: E.tensor_copy(d, s6),
                         seq=False)
                else:
                    P.op("vector", lambda E, d=dst, s6=src6: E.tensor_add(d, d, s6),
                         seq=False)
                vtk = P.tok("vector")
                bk.renew(o, ("ve", vtk))
            stage_busy.clear()
        P.barrier()

    def ffn(g):
        split_rel = None
        for cyc in range(3):
            inh = xnf_h if cyc == 0 else outh
            inl = xnf_l if cyc == 0 else outl
            forder = list(range(8)) if split_rel is None else [0, 4, 1, 5, 2, 6, 3, 7]
            for fi_, f in enumerate(forder):
                sl, _ = wstream(scr["cn_w1"][0, 128 * f:128 * (f + 1), :],
                                scr["cn_w1"][1, 128 * f:128 * (f + 1), :], 1024)
                if split_rel is not None:
                    P.wait("tensor", "ve", split_rel[f][0])
                    P.wait("tensor", "gp", split_rel[f][1])
                for o in range(8):
                    mm3(o, ps[o][:, 0:512],
                        wbh[:, sl, 128 * o:128 * (o + 1)], wbl[:, sl, 128 * o:128 * (o + 1)],
                        flat(inh[:, f, :, :]), flat(inl[:, f, :, :]), last=(fi_ == 7))
                wmark(sl)
            P.wait("scalar", "pe")
            ev1 = []
            for o in range(8):
                h_ve, h_gp, sct = dual_evac(ps[o][:, 0:512], flat(qf_h[:, o, :, :]),
                                            flat(qf_l[:, o, :, :]), stg_slot=o % 2,
                                            func=AF.Gelu, bias=cnb1[:, o:o + 1])
                bk.renew(o, ("sc", sct))
                ev1.append((h_ve, h_gp))
            for f in range(8):
                sl, _ = wstream(scr["cn_w2"][0, 128 * f:128 * (f + 1), :],
                                scr["cn_w2"][1, 128 * f:128 * (f + 1), :], 512)
                P.wait("tensor", "ve", ev1[f][0])
                P.wait("tensor", "gp", ev1[f][1])
                for o in range(4):
                    mm3(o, ps[o][:, 0:512],
                        wbh[:, sl, 128 * o:128 * (o + 1)], wbl[:, sl, 128 * o:128 * (o + 1)],
                        flat(qf_h[:, f, :, :]), flat(qf_l[:, f, :, :]), last=(f == 7))
                wmark(sl)
            P.wait("scalar", "pe")
            ev2 = []
            for o in range(4):
                h_ve, h_gp, sct = dual_evac(ps[o][:, 0:512], flat(gqf_h[:, o, :, :]),
                                            flat(gqf_l[:, o, :, :]), stg_slot=o % 2,
                                            func=AF.Gelu, bias=cnb2[:, o:o + 1])
                bk.renew(o, ("sc", sct))
                ev2.append((h_ve, h_gp))
            for f in range(4):
                sl, _ = wstream(scr["cn_w3"][0, 128 * f:128 * (f + 1), :],
                                scr["cn_w3"][1, 128 * f:128 * (f + 1), :], 256)
                P.wait("tensor", "ve", ev2[f][0])
                P.wait("tensor", "gp", ev2[f][1])
                for o in range(2):
                    mm3(o, ps[o][:, 0:512],
                        wbh[:, sl, 128 * o:128 * (o + 1)], wbl[:, sl, 128 * o:128 * (o + 1)],
                        flat(gqf_h[:, f, :, :]), flat(gqf_l[:, f, :, :]), last=(f == 3))
                wmark(sl)
            P.wait("scalar", "pe")
            ev3 = []
            for o in range(2):
                h_ve, h_gp, sct = dual_evac(ps[o][:, 0:512], flat(xnf_h[:, o, :, :]),
                                            flat(xnf_l[:, o, :, :]), stg_slot=o % 2,
                                            func=AF.Gelu, bias=cnb3[:, o:o + 1])
                bk.renew(o, ("sc", sct))
                ev3.append((h_ve, h_gp))
            for f in range(2):
                P.wait("tensor", "ve", ev3[f][0])
                P.wait("tensor", "gp", ev3[f][1])
                mm3(2, ps[2][0:1, 0:512], cn4h[:, f, :], cn4l[:, f, :],
                    flat(xnf_h[:, f, :, :]), flat(xnf_l[:, f, :, :]), last=(f == 1))
            tk = P.tok("tensor")
            P.wait("scalar", "pe", tk)
            P.op("scalar", lambda E: E.activation(tf32, ps[2][0:1, 0:512],
                                                  AF.Sigmoid, bias=cnb4[:]))
            sct = P.tok("scalar")
            bk.renew(2, ("sc", sct))
            P.wait("vector", "sc", sct)
            P.op("vector", lambda E: E.tensor_scalar(
                c1row, tf32, ab32[0:1, 1:2], ab32[0:1, 0:1], ALU.mult, ALU.add))
            vtk = P.tok("vector")
            P.wait("tensor", "ve", vtk)
            mm(3, ps[3][:, 0:512], ones32[:], c1row, last=True)
            tk = P.tok("tensor")
            P.wait("scalar", "pe", tk)
            P.op("scalar", lambda E: E.copy(c32, ps[3][:, 0:512]))
            sct = P.tok("scalar")
            bk.renew(3, ("sc", sct))
            P.wait("vector", "sc", sct)
            P.wait("gpsimd", "sc", sct)
            pairs = [(0, ALU.add, ALU.subtract), (1, ALU.add, ALU.subtract),
                     (2, ALU.subtract, ALU.add), (3, ALU.subtract, ALU.add)]
            va_toks = []
            for pi, (fa, sa, sbn) in enumerate(pairs):
                fb = fa + 4
                eng = "vector" if pi < 3 else "gpsimd"
                ta = tsc0 if pi < 3 else stg[:, 0, :]
                tb = tsc1 if pi < 3 else stg[:, 1, :]
                P.op(eng, lambda E, a=fa, t=ta: E.tensor_tensor(
                    t, c32, flat(outr[:, a, :, :]), ALU.mult))
                P.op(eng, lambda E, b=fb, t=tb: E.tensor_tensor(
                    t, c32, flat(outr[:, b, :, :]), ALU.mult), seq=False)
                P.op(eng, lambda E, b=fb, t=ta, o=sbn: E.tensor_tensor(
                    flat(outr[:, b, :, :]), flat(outr[:, b, :, :]), t, o))
                P.op(eng, lambda E, a=fa, t=tb, o=sa: E.tensor_tensor(
                    flat(outr[:, a, :, :]), flat(outr[:, a, :, :]), t, o),
                    seq=False)
                if pi < 3 and cyc < 2:
                    P.op("vector", lambda E, f=fa: E.tensor_copy(
                        flat(outh[:, f, :, :]), flat(outr[:, f, :, :])))
                    P.op("vector", lambda E, f=fb: E.tensor_copy(
                        flat(outh[:, f, :, :]), flat(outr[:, f, :, :])), seq=False)
                    va_toks.append((fa, fb, P.tok("vector")))
            gpt_tw = P.tok("gpsimd")
            stage_busy[0] = gpt_tw
            stage_busy[1] = gpt_tw
            if cyc < 2:
                P.wait("vector", "gp", gpt_tw)
                P.op("vector", lambda E: E.tensor_copy(
                    flat(outh[:, 3, :, :]), flat(outr[:, 3, :, :])))
                P.op("vector", lambda E: E.tensor_copy(
                    flat(outh[:, 7, :, :]), flat(outr[:, 7, :, :])), seq=False)
                v2 = P.tok("vector")
                split_toks = {}
                for fa, fb, va in va_toks:
                    P.wait("gpsimd", "ve", va)
                    P.op("gpsimd", lambda E, f=fa: E.tensor_sub(
                        flat(outl[:, f, :, :]), flat(outr[:, f, :, :]),
                        flat(outh[:, f, :, :])))
                    P.op("gpsimd", lambda E, f=fb: E.tensor_sub(
                        flat(outl[:, f, :, :]), flat(outr[:, f, :, :]),
                        flat(outh[:, f, :, :])), seq=False)
                    gt = P.tok("gpsimd")
                    split_toks[fa] = (va, gt)
                    split_toks[fb] = (va, gt)
                P.wait("gpsimd", "ve", v2)
                for f in (3, 7):
                    P.op("gpsimd", lambda E, ff=f: E.tensor_sub(
                        flat(outl[:, ff, :, :]), flat(outr[:, ff, :, :]),
                        flat(outh[:, ff, :, :])), seq=(f == 3))
                gp2 = P.tok("gpsimd")
                split_toks[3] = (v2, gp2)
                split_toks[7] = (v2, gp2)
                split_rel = split_toks
        P.barrier()
        for ti in range(GT):
            for half in range(2):
                pe_take(half)
                for fi in range(4):
                    f = 4 * half + fi
                    P.op("tensor", lambda E, t=ti, ff=f, h=half, j=fi: E.transpose(
                        ps[h][:, 128 * j:128 * (j + 1)], outr[:, ff, t, :], idn[:]))
                tk = P.tok("tensor")
                P.wait("vector", "pe", tk)
                P.op("vector", lambda E, t=ti, h=half: E.tensor_add(
                    final[:, t, 512 * h:512 * (h + 1)],
                    final[:, t, 512 * h:512 * (h + 1)], ps[h][:, 0:512]))
                vtk = P.tok("vector")
                bk.renew(half, ("ve", vtk))
            pt = GT * g + ti
            slot = ti % 2
            if slot in xslot_tok:
                P.wait("sync", "ve", xslot_tok[slot])
            xtok = P.dma(xt[:, slot, :], xb_h[128 * pt:128 * (pt + 1), :])
            P.wait("vector", "dS", xtok)
            P.op("vector", lambda E, t=ti, s=slot: E.tensor_add(
                final[:, t, :], final[:, t, :], xt[:, s, :]))
            vtk = P.tok("vector")
            xslot_tok[slot] = vtk
            P.wait("sync", "ve", vtk)
            P.dma(y_h[128 * pt:128 * (pt + 1), :], final[:, ti, :], sem="st")

    if DEBUG_DUMPS:
        P.dma(dbg["d_stats"][:, 0:NT], ssum[:], sem="st")
        P.dma(dbg["d_stats"][:, NT:2 * NT], ssq[:], sem="st")
        P.dma(dbg["d_stats"][:, 2 * NT:3 * NT], mu_all[:], sem="st")
        P.dma(dbg["d_stats"][:, 3 * NT:4 * NT], r_all[:], sem="st")
        for nm, hi, lo, shp in [("d_pfm", pfm_h, pfm_l, None),
                                ("d_gk", gk_h, gk_l, None),
                                ("d_gv", gv_h, gv_l, None)]:
            t32 = final[0:hi.shape[0], 0, 0:1]
        P.op("vector", lambda E: E.tensor_add(
            outr[:, 0:4, :, :].rearrange("p a t c -> p (a t c)")[:, 0:8 * NLS]
            .rearrange("p (a b) -> p a b", b=NLS),
            pfm_h[:], pfm_l[:]))
        vtk = P.tok("vector")
        P.wait("sync", "ve", vtk)
        P.dma(dbg["d_pfm"][:], outr[:, 0:4, :, :].rearrange("p a t c -> p (a t c)")
              [:, 0:8 * NLS].rearrange("p (a b) -> p a b", b=NLS), sem="st")
        P.op("vector", lambda E: E.tensor_add(
            outr[:, 4:6, :, :].rearrange("p a t c -> p (a t c)")[:, 0:4 * NLS]
            .rearrange("p (a b) -> p a b", b=NLS), gk_h[:], gk_l[:]))
        vtk = P.tok("vector")
        P.wait("sync", "ve", vtk)
        P.dma(dbg["d_gk"][:], outr[:, 4:6, :, :].rearrange("p a t c -> p (a t c)")
              [:, 0:4 * NLS].rearrange("p (a b) -> p a b", b=NLS), sem="st")
        P.op("vector", lambda E: E.tensor_add(
            outr[0:NLS, 6:8, :, :].rearrange("p a t c -> p (a t c)")[:, 0:LH * NLS]
            .rearrange("p (a b) -> p a b", b=NLS), gv_h[:], gv_l[:]))
        vtk = P.tok("vector")
        P.wait("sync", "ve", vtk)
        P.dma(dbg["d_gv"][:], outr[0:NLS, 6:8, :, :].rearrange("p a t c -> p (a t c)")
              [:, 0:LH * NLS].rearrange("p (a b) -> p a b", b=NLS), sem="st")
        P.op("vector", lambda E: E.tensor_copy(arena[0:1, 0:1], gf32[:]))
        P.op("vector", lambda E: E.tensor_copy(arena[0:1, 1:3], ab32[:]))
        vtk = P.tok("vector")
        P.wait("sync", "ve", vtk)
        P.dma(dbg["d_misc"][:], arena[0:1, 0:8], sem="st")
        P.barrier()

    proj_tiles([32, 33], False, -1)
    P.barrier()
    for g in range(NG):
        proj_tiles(list(range(GT * g, GT * (g + 1))), True, g)
        P.barrier()
        if DEBUG_DUMPS and g == 0:
            P.op("vector", lambda E: E.tensor_add(
                outr[:].rearrange("p a t c -> p (a t c)"),
                xnf_h[:].rearrange("p a t c -> p (a t c)"),
                xnf_l[:].rearrange("p a t c -> p (a t c)")))
            vtk = P.tok("vector")
            P.wait("sync", "ve", vtk)
            P.dma(dbg["d_xnf"][:], outr[:], sem="st")
            P.wait("vector", "st", P.val("st"))
            P.op("vector", lambda E: E.tensor_add(
                outr[:].rearrange("p a t c -> p (a t c)"),
                qf_h[:].rearrange("p a t c -> p (a t c)"),
                qf_l[:].rearrange("p a t c -> p (a t c)")))
            vtk = P.tok("vector")
            P.wait("sync", "ve", vtk)
            P.dma(dbg["d_qf"][:], outr[:], sem="st")
            P.dma(dbg["d_gtm"][:], g_tm[:], sem="st")
            P.dma(dbg["d_kfh"][:], kf_h[:], sem="st")
            P.dma(dbg["d_kfl"][:], kf_l[:], sem="st")
            P.dma(dbg["d_vth"][:], vt_h[:], sem="st")
            P.dma(dbg["d_vtl"][:], vt_l[:], sem="st")
            P.barrier()
        attention(g)
        if DEBUG_DUMPS and g == 0:
            vtk = P.tok("vector")
            P.wait("sync", "ve", vtk)
            P.dma(dbg["d_ghalf"][:], ghalf[:], sem="st")
            P.barrier()
        ffn(g)
        P.barrier()
    P.wait("sync", "st", P.val("st"))
    P.finalize_chain()

    blk = es.enter_context(nc.Block())

    def make_section(ename):
        def section(E):
            for c in P.cmds[ename]:
                if c[0] == "drain":
                    E.drain()
                elif c[0] == "wait":
                    E.wait_ge(sems[c[1]], c[2])
                elif c[0] == "dma":
                    E.dma_start(out=c[1], in_=c[2]).then_inc(sems[c[3]], 16)
                else:
                    inst = c[1](E)
                    if c[2] is not None:
                        inst.then_inc(sems[c[2][0]], 1)
        return section

    blk.sync(make_section("sync"))
    blk.tensor(make_section("tensor"))
    blk.vector(make_section("vector"))
    blk.scalar(make_section("scalar"))
    blk.gpsimd(make_section("gpsimd"))
    es.close()
    return nc


def _host_inputs(inputs):
    x = np.ascontiguousarray(np.asarray(inputs["L0_attn_x"], dtype=np.float32))
    glw = np.asarray(inputs["L0_attn_gate_local_w"], dtype=np.float32)
    grw = np.asarray(inputs["L0_attn_gate_remote_w"], dtype=np.float32)
    gw = np.zeros((1024, 32), np.float32)
    for hh in range(16):
        gw[64 * hh:64 * (hh + 1), hh] = glw[hh]
        gw[64 * hh:64 * (hh + 1), 16 + hh] = grw[hh]
    mtri = np.zeros((128, 2, 128), np.float32)
    mtri[:, 0, :] = (np.arange(128)[:, None] <= np.arange(128)[None, :])
    mtri[:, 1, :] = (np.arange(128)[:, None] > np.arange(128)[None, :])
    idn = np.eye(128, dtype=np.float32)
    shared = {
        "wq": inputs["L0_attn_wq"], "wk": inputs["L0_attn_wk"], "wv": inputs["L0_attn_wv"],
        "wgq": inputs["L0_attn_wgq"], "wgk": inputs["L0_attn_wgk"],
        "wgv": inputs["L0_attn_wgv"],
        "r_out": inputs["L0_attn_r_out_w"], "w_out": inputs["L0_attn_w_out"],
        "cn_w1": inputs["L0_ffn_cn_w1"], "cn_w2": inputs["L0_ffn_cn_w2"],
        "cn_w3": inputs["L0_ffn_cn_w3"], "cn_w4": inputs["L0_ffn_cn_w4"],
        "gc_w1": inputs["L0_ffn_gc_w1"], "gc_w2": inputs["L0_ffn_gc_w2"],
        "gc_w3": inputs["L0_ffn_gc_w3"], "gw": gw,
        "cn_b1": np.asarray(inputs["L0_ffn_cn_b1"]).reshape(8, 128).T,
        "cn_b2": np.asarray(inputs["L0_ffn_cn_b2"]).reshape(4, 128).T,
        "cn_b3": np.asarray(inputs["L0_ffn_cn_b3"]).reshape(2, 128).T,
        "cn_b4": np.asarray(inputs["L0_ffn_cn_b4"]).reshape(1, 1),
        "gc_b1": np.asarray(inputs["L0_ffn_gc_b1"]).reshape(4, 128).T,
        "gc_b2": np.asarray(inputs["L0_ffn_gc_b2"]).reshape(2, 128).T,
        "gc_b3": np.asarray(inputs["L0_ffn_gc_b3"]).reshape(1, 1),
        "ar": np.asarray(inputs["L0_ffn_adaptive_range"]).reshape(1, 1),
        "mtri": mtri, "idn": idn,
    }
    shared = {k: np.ascontiguousarray(np.asarray(v, dtype=np.float32))
              for k, v in shared.items()}
    in_maps = []
    for core in range(8):
        b, q = core // 4, core % 4
        xb_full = x[b]
        own_orig = list(range(16 * q, 16 * q + 16))
        halo_orig = 16 * q - 1 if q > 0 else None
        rest_orig = [i for i in range(64) if i not in own_orig and i != halo_orig]
        chunks = [xb_full[256 * i:256 * (i + 1)] for i in own_orig]
        chunks.append(xb_full[4096 * q - 256:4096 * q] if q > 0
                      else np.zeros((256, 1024), np.float32))
        chunks += [xb_full[256 * i:256 * (i + 1)] for i in rest_orig]
        origs = own_orig + [halo_orig] + rest_orig
        while len(origs) < NLS:
            chunks.append(np.zeros((256, 1024), np.float32))
            origs.append(None)
        xb_perm = np.ascontiguousarray(np.concatenate(chunks, 0))
        medge = (np.ones((128, 2), np.float32) if q > 0
                 else np.zeros((128, 2), np.float32))
        mlat = np.zeros((NLS, 4096), np.float32)
        tpos = (4096 * q + np.arange(4096)) // 256
        for j, o in enumerate(origs):
            if o is not None:
                mlat[j] = (o <= tpos)
        m = dict(shared)
        m["xb"] = xb_perm
        m["medge"] = np.ascontiguousarray(medge)
        m["mlat"] = np.ascontiguousarray(mlat)
        in_maps.append(m)
    return in_maps


def kernel(**inputs):
    in_maps = _host_inputs(inputs)
    nc = build_program()
    res = run_bass_kernel_spmd(nc, in_maps, core_ids=list(range(8)))
    out = np.zeros((2, 16384, 1024), np.float32)
    for core in range(8):
        b, q = core // 4, core % 4
        out[b, 4096 * q:4096 * (q + 1)] = res.results[core]["y"]
    return out


# version 13 (final)
# speedup vs baseline: 1.0027x; 1.0027x over previous
import numpy as np
from contextlib import ExitStack
import concourse.bass as bass
import concourse.mybir as mybir
from concourse.bass_utils import run_bass_kernel_spmd

F32 = mybir.dt.float32
BF16 = mybir.dt.bfloat16
AF = mybir.ActivationFunctionType
ALU = mybir.AluOpType

D = 1024
NH, HD, LH = 16, 64, 8
NLS = 65
NT = 130
GT = 4
NG = 8
KVR = 6
EPS = 1e-5

DEBUG_DUMPS = False

ENGINES = ("sync", "tensor", "vector", "scalar", "gpsimd")
ESEM = {"sync": "sy", "tensor": "pe", "vector": "ve", "scalar": "sc", "gpsimd": "gp"}


class Plan:

    def __init__(self, ndmasem=24):
        self.cmds = {e: [] for e in ENGINES}
        self.cnt = {}
        self.wm = {e: {} for e in ENGINES}
        self.ndmasem = ndmasem
        self.dma_i = 0

    def val(self, sem):
        return self.cnt.get(sem, 0)

    def _bump(self, sem, amt):
        self.cnt[sem] = self.cnt.get(sem, 0) + amt
        return self.cnt[sem]

    def wait(self, eng, sem, v=None):
        if isinstance(v, list):
            for t in v:
                self.wait(eng, sem, t)
            return
        if isinstance(v, tuple):
            sem, v = v
        v = self.val(sem) if v is None else v
        if v is None or v <= 0 or v <= self.wm[eng].get(sem, 0):
            return
        self.wm[eng][sem] = v
        self.cmds[eng].append(("wait", sem, v))

    def op(self, eng, fn, inc=False, seq=True):
        self.cmds[eng].append(["op", fn, None, seq])
        if inc:
            return self.tok(eng)
        return None

    def tok(self, eng):
        sem = ESEM[eng]
        last_op = None
        trailing = False
        for c in reversed(self.cmds[eng]):
            if c[0] == "op" and isinstance(c, list):
                last_op = c
                break
            trailing = True
        if last_op is not None and last_op[2] is None:
            last_op[2] = (sem, self._bump(sem, 1))
        if last_op is not None and not trailing:
            return last_op[2][1]
        self.cmds[eng].append(["op", lambda E: E.nop(), (sem, self._bump(sem, 1))])
        return self.cmds[eng][-1][2][1]

    def dma(self, dst, src, sem="dS"):
        if sem == "st":
            v = self._bump("st", 16)
            self.cmds["sync"].append(("dma", dst, src, "st"))
            return ("st", v)
        s = f"d{self.dma_i % self.ndmasem}"
        self.dma_i += 1
        prev = self.val(s)
        if prev:
            self.wait("sync", s, prev)
        v = self._bump(s, 16)
        self.cmds["sync"].append(("dma", dst, src, s))
        return (s, v)

    def finalize_chain(self):
        for e in ("vector", "scalar", "gpsimd"):
            out = []
            prev_was_op = False
            for c in self.cmds[e]:
                if isinstance(c, list) and c[0] == "op":
                    if prev_was_op and (len(c) < 4 or c[3]):
                        out.append(("drain",))
                    prev_was_op = True
                    out.append(c)
                else:
                    out.append(c)
            self.cmds[e] = out

    def barrier(self):
        for e in ENGINES:
            for i in range(self.ndmasem):
                s = f"d{i}"
                if self.val(s):
                    self.wait(e, s, self.val(s))
        snap = {}
        for e in ENGINES:
            self.tok(e)
            snap[ESEM[e]] = self.val(ESEM[e])
        for e in ENGINES:
            for f in ENGINES:
                if f != e:
                    self.wait(e, ESEM[f], snap[ESEM[f]])


class Banks:

    def __init__(self):
        self.fresh = [True] * 8
        self.rel = [None] * 8

    def start(self, i):
        s = self.fresh[i]
        self.fresh[i] = False
        r, self.rel[i] = self.rel[i], None
        return s, r

    def renew(self, i, rel=None):
        self.fresh[i] = True
        self.rel[i] = rel

    def take(self, i):
        self.fresh[i] = False
        r, self.rel[i] = self.rel[i], None
        return r


def build_program():
    nc = bass.Bass("TRN2", target_bir_lowering=False, debug=False)

    xb_h = nc.dram_tensor("xb", [NT * 128, D], F32, kind="ExternalInput").ap()
    y_h = nc.dram_tensor("y", [4096, D], F32, kind="ExternalOutput").ap()
    dbg = {}
    if DEBUG_DUMPS:
        for nm, shp, dt in [("d_stats", [128, 4 * NT], F32),
                            ("d_pfm", [128, 8, NLS], F32),
                            ("d_gk", [128, 4, NLS], F32),
                            ("d_gv", [NLS, LH, NLS], F32),
                            ("d_misc", [1, 8], F32),
                            ("d_xnf", [128, 8, GT, 128], F32),
                            ("d_qf", [128, 8, GT, 128], F32),
                            ("d_gtm", [128, GT, 32], F32),
                            ("d_ghalf", [128, GT, 512], F32),
                            ("d_kfh", [128, 8, KVR, 128], BF16),
                            ("d_kfl", [128, 8, KVR, 128], BF16),
                            ("d_vth", [128, KVR, NH, 65], BF16),
                            ("d_vtl", [128, KVR, NH, 65], BF16)]:
            dbg[nm] = nc.dram_tensor(nm, shp, dt, kind="ExternalOutput").ap()
    win = {}
    for nm, shp in [("wq", [D, D]), ("wk", [D, D]), ("wv", [D, D]),
                    ("wgq", [D, 512]), ("wgk", [D, 512]), ("wgv", [D, 512]),
                    ("r_out", [512, D]), ("w_out", [D, D]),
                    ("cn_w1", [D, D]), ("cn_w2", [D, 512]), ("cn_w3", [512, 256]),
                    ("cn_w4", [256, 1]),
                    ("gc_w1", [D, 512]), ("gc_w2", [512, 256]), ("gc_w3", [256, 1]),
                    ("gw", [D, 32])]:
        win[nm] = nc.dram_tensor(nm, shp, F32, kind="ExternalInput").ap()
    bin_ = {}
    for nm, shp in [("cn_b1", [128, 8]), ("cn_b2", [128, 4]), ("cn_b3", [128, 2]),
                    ("cn_b4", [1, 1]), ("gc_b1", [128, 4]), ("gc_b2", [128, 2]),
                    ("gc_b3", [1, 1]), ("ar", [1, 1])]:
        bin_[nm] = nc.dram_tensor(nm, shp, F32, kind="ExternalInput").ap()
    mtri_h = nc.dram_tensor("mtri", [128, 2, 128], F32, kind="ExternalInput").ap()
    medge_h = nc.dram_tensor("medge", [128, 2], F32, kind="ExternalInput").ap()
    mlat_h = nc.dram_tensor("mlat", [NLS, 4096], F32, kind="ExternalInput").ap()
    idn_h = nc.dram_tensor("idn", [128, 128], F32, kind="ExternalInput").ap()

    scr = {}
    for nm in ["wq", "wk", "wv", "wgq", "r_out", "w_out", "cn_w1", "cn_w2", "cn_w3"]:
        R, C = win[nm].shape
        scr[nm] = nc.dram_tensor("scr_" + nm, [2, R, C], BF16, kind="Internal").ap()

    es = ExitStack()

    def sb(name, shape, dt):
        return es.enter_context(nc.sbuf_tensor(name, shape, dt))

    idn = sb("idn_s", [128, 128], F32)
    mtri = sb("mtri_s", [128, 2, 128], F32)
    medge = sb("medge_s", [128, 2], F32)
    mlat_s = sb("mlat_sl", [NLS, 512], F32)
    cnb1 = sb("cnb1", [128, 8], F32)
    cnb2 = sb("cnb2", [128, 4], F32)
    cnb3 = sb("cnb3", [128, 2], F32)
    cnb4 = sb("cnb4", [1, 1], F32)
    gcb1 = sb("gcb1", [128, 4], F32)
    gcb2 = sb("gcb2", [128, 2], F32)
    gcb3 = sb("gcb3", [1, 1], F32)
    ar_s = sb("ar_s", [1, 1], F32)
    ab32 = sb("ab32", [1, 2], F32)
    ones32 = sb("ones32", [1, 128], F32)
    gw_h = sb("gw_h", [128, 8, 32], BF16)
    gw_l = sb("gw_l", [128, 8, 32], BF16)
    cn4h = sb("cn4h", [128, 2, 1], BF16)
    cn4l = sb("cn4l", [128, 2, 1], BF16)
    ssum = sb("ssum", [128, NT], F32)
    ssq = sb("ssq", [128, NT], F32)
    mu_all = sb("mu_all", [128, NT], F32)
    r_all = sb("r_all", [128, NT], F32)
    sd_t = sb("sd_t", [128, 1], F32)
    pm32 = sb("pm32", [128, NLS], F32)
    pm_h = sb("pm_h", [128, NLS], BF16)
    pm_l = sb("pm_l", [128, NLS], BF16)
    mu_h = sb("mu_h", [128, 1], BF16)
    mu_l = sb("mu_l", [128, 1], BF16)
    corr = sb("corr", [NLS, 1], F32)
    pfm_h = sb("pfm_h", [128, 8, NLS], BF16)
    pfm_l = sb("pfm_l", [128, 8, NLS], BF16)
    gk_h = sb("gk_h", [128, 4, NLS], BF16)
    gk_l = sb("gk_l", [128, 4, NLS], BF16)
    gv_h = sb("gv_h", [NLS, LH, NLS], BF16)
    gv_l = sb("gv_l", [NLS, LH, NLS], BF16)
    gc32 = sb("gc32", [128, 8], F32)
    gc_h = sb("gc_h", [128, 8], BF16)
    gc_l = sb("gc_l", [128, 8], BF16)
    h1gc = sb("h1gc", [128, 4], F32)
    h1gc_h = sb("h1gc_h", [128, 4], BF16)
    h1gc_l = sb("h1gc_l", [128, 4], BF16)
    h2gc = sb("h2gc", [128, 2], F32)
    h2gc_h = sb("h2gc_h", [128, 2], BF16)
    h2gc_l = sb("h2gc_l", [128, 2], BF16)
    gf32 = sb("gf32", [1, 1], F32)
    xt = sb("xt", [128, 2, D], F32)
    wbh = sb("wbh", [128, 2, D], BF16)
    wbl = sb("wbl", [128, 2, D], BF16)
    arena = sb("arena", [128, 2048], F32)
    xnf_h = sb("xnf_h", [128, 8, GT, 128], BF16)
    xnf_l = sb("xnf_l", [128, 8, GT, 128], BF16)
    qf_h = sb("qf_h", [128, 8, GT, 128], BF16)
    qf_l = sb("qf_l", [128, 8, GT, 128], BF16)
    gqf_h = sb("gqf_h", [128, 4, GT, 128], BF16)
    gqf_l = sb("gqf_l", [128, 4, GT, 128], BF16)
    kf_h = sb("kf_h", [128, 8, KVR, 128], BF16)
    kf_l = sb("kf_l", [128, 8, KVR, 128], BF16)
    vt_h = sb("vt_h", [128, KVR, NH, 65], BF16)
    vt_l = sb("vt_l", [128, KVR, NH, 65], BF16)
    g_tm = sb("g_tm", [128, GT, 32], F32)
    stg = sb("stg", [128, 2, 512], F32)
    final = sb("final", [128, GT, D], F32)
    outr = sb("outr", [128, 8, GT, 128], F32)
    outh = sb("outh", [128, 8, GT, 128], BF16)
    outl = sb("outl", [128, 8, GT, 128], BF16)
    pl_h = sb("pl_h", [128, 2, 384], BF16)
    pl_l = sb("pl_l", [128, 2, 384], BF16)
    plat_h = sb("plat_h", [NLS, 512], BF16)
    plat_l = sb("plat_l", [NLS, 512], BF16)
    remf_h = sb("remf_h", [128, 4, GT, 128], BF16)
    remf_l = sb("remf_l", [128, 4, GT, 128], BF16)
    ghalf = sb("ghalf", [128, GT, 512], F32)
    gfm_h = sb("gfm_h", [128, 4, GT, 128], BF16)
    gfm_l = sb("gfm_l", [128, 4, GT, 128], BF16)
    tiny = sb("tiny", [128, 16], F32)

    gstg = stg[0:32, 1, 0:512]

    def xh_t(slot):
        return outh[:, 2 * slot:2 * slot + 2, :, :].rearrange("p a t c -> p (a t c)")

    def xl_t(slot):
        return outl[:, 2 * slot:2 * slot + 2, :, :].rearrange("p a t c -> p (a t c)")

    def wtmp(slot):
        return outr[:, 2 * slot:2 * slot + 2, :, :].rearrange("p a t c -> p (a t c)")

    def xnt(slot):
        return arena[:, 1024 * slot:1024 * (slot + 1)]

    rem_tm = ghalf
    plat32 = arena[0:NLS, 0:512]
    p32bufs = (arena[:, 512:896], arena[:, 1536:1920])
    p32_rd = {0: None, 1: None}
    tsc0 = arena[:, 0:512]
    tsc1 = arena[:, 512:1024]
    c32 = arena[:, 1024:1536]
    tf32 = arena[0:1, 1536:2048]
    c1row = arena[0:1, 0:512]

    ps = [es.enter_context(nc.psum_tensor(f"ps{i}", [128, 512], F32)) for i in range(8)]
    bk = Banks()
    sems = {}
    for s in (["st", "pe", "ve", "sc", "gp", "sy"] + [f"d{i}" for i in range(24)]):
        sems[s] = es.enter_context(nc.semaphore(s))

    P = Plan()
    stage_busy = {}

    def mm(bank_i, out_ap, lhsT, rhs, last=False):
        st_, rel = bk.start(bank_i)
        if rel is not None:
            P.wait("tensor", rel[0], rel[1])
        P.op("tensor", lambda E, o=out_ap, l=lhsT, r=rhs, s=st_, e=last:
             E.matmul(o, l, r, start=s, stop=e, skip_group_check=True))

    def mm3(bank_i, out_ap, lh, ll, rh, rl, last=False):
        mm(bank_i, out_ap, lh, rh)
        mm(bank_i, out_ap, lh, rl)
        mm(bank_i, out_ap, ll, rh, last=last)

    def dual_evac(bank_view, hi, lo, stg_slot=None, stage=None, func=None,
                  bias=None, scale=None):
        if stage is None:
            n = 1
            for d in bank_view.shape[1:]:
                n *= d
            stage = stg[:, stg_slot, 0:n]
        if stg_slot is not None and stg_slot in stage_busy:
            P.wait("scalar", "gp", stage_busy[stg_slot])
        if func is None:
            P.op("scalar", lambda E, o=stage, i=bank_view: E.copy(o, i), seq=False)
        else:
            kw = {}
            if bias is not None:
                kw["bias"] = bias
            if scale is not None:
                kw["scale"] = scale
            P.op("scalar", lambda E, o=stage, i=bank_view, f=func, k=kw:
                 E.activation(o, i, f, **k), seq=False)
        sct = P.tok("scalar")
        P.wait("vector", "sc", sct)
        P.op("vector", lambda E, o=hi, i=stage: E.tensor_copy(o, i), seq=False)
        vet = P.tok("vector")
        P.wait("gpsimd", "ve", vet)
        P.op("gpsimd", lambda E, o=lo, a=stage, b=hi: E.tensor_sub(o, a, b), seq=False)
        gpt = P.tok("gpsimd")
        if stg_slot is not None:
            stage_busy[stg_slot] = gpt
        return vet, gpt, sct

    def pe_guard():
        P.wait("tensor", "sc")
        P.wait("tensor", "ve")
        P.wait("tensor", "gp")

    def pe_take(i):
        rel = bk.take(i)
        if rel is not None:
            P.wait("tensor", rel[0], rel[1])

    P.dma(idn[:], idn_h)
    P.dma(mtri[:], mtri_h)
    P.dma(medge[:], medge_h)
    P.dma(cnb1[:], bin_["cn_b1"])
    P.dma(cnb2[:], bin_["cn_b2"])
    P.dma(cnb3[:], bin_["cn_b3"])
    P.dma(cnb4[:], bin_["cn_b4"])
    P.dma(gcb1[:], bin_["gc_b1"])
    P.dma(gcb2[:], bin_["gc_b2"])
    P.dma(gcb3[:], bin_["gc_b3"])
    P.dma(ar_s[:], bin_["ar"])
    P.op("vector", lambda E: E.memset(ones32[:], 1.0))
    for sl in range(KVR):
        P.op("vector", lambda E, s=sl: E.memset(vt_h[:, s, :, 64:65], 1.0))
        P.op("vector", lambda E, s=sl: E.memset(vt_l[:, s, :, 64:65], 0.0))
    P.op("vector", lambda E: E.memset(gv_h[:, :, 64:65], 1.0))
    P.op("vector", lambda E: E.memset(gv_l[:, :, 64:65], 0.0))
    P.op("vector", lambda E: E.memset(ssum[:], 0.0))
    P.op("vector", lambda E: E.memset(ssq[:], 0.0))
    P.tok("vector")

    slot_state = {}

    def split_chunk(src_ap, rows, C, slot, hi_dst, lo_dst, to_scratch):
        if slot in slot_state:
            vt0, gt0, dt0 = slot_state[slot]
            P.wait("sync", "ve", vt0)
            P.wait("sync", "gp", gt0)
            if dt0:
                P.wait("vector", "dS", dt0)
                P.wait("gpsimd", "dS", dt0)
                P.wait("scalar", "dS", dt0)
        t = P.dma(wtmp(slot)[0:rows, 0:C], src_ap)
        P.wait("vector", "dS", t)
        hdst = xh_t(slot)[0:rows, 0:C] if to_scratch else hi_dst
        ldst = xl_t(slot)[0:rows, 0:C] if to_scratch else lo_dst
        P.op("vector", lambda E, o=hdst, i=wtmp(slot)[0:rows, 0:C]: E.tensor_copy(o, i),
             seq=False)
        vt = P.tok("vector")
        P.wait("gpsimd", "ve", vt)
        P.op("gpsimd", lambda E, o=ldst, a=wtmp(slot)[0:rows, 0:C], b=hdst:
             E.tensor_sub(o, a, b), seq=False)
        gt_ = P.tok("gpsimd")
        dt = 0
        if to_scratch:
            P.wait("sync", "ve", vt)
            P.wait("sync", "gp", gt_)
            t1 = P.dma(hi_dst, hdst)
            t2 = P.dma(lo_dst, ldst)
            dt = [t1, t2]
        slot_state[slot] = (vt, gt_, dt)
        return vt, gt_

    ci_all = 0
    for nm in ["wq", "wk", "wv", "wgq", "r_out", "w_out", "cn_w1", "cn_w2", "cn_w3"]:
        R, C = win[nm].shape
        for ci in range((R + 127) // 128):
            rows = min(128, R - 128 * ci)
            split_chunk(win[nm][128 * ci:128 * ci + rows, :], rows, C, ci_all % 2,
                        scr[nm][0, 128 * ci:128 * ci + rows, :],
                        scr[nm][1, 128 * ci:128 * ci + rows, :], True)
            ci_all += 1
    for ci in range(8):
        split_chunk(win["gw"][128 * ci:128 * (ci + 1), :], 128, 32, ci_all % 2,
                    gw_h[:, ci, :], gw_l[:, ci, :], False)
        ci_all += 1
    for ci in range(2):
        split_chunk(win["cn_w4"][128 * ci:128 * (ci + 1), :], 128, 1, ci_all % 2,
                    cn4h[:, ci, :], cn4l[:, ci, :], False)
        ci_all += 1
    P.barrier()

    pe_tile = {}
    for t in range(NT):
        slot = t % 2
        if t >= 2:
            P.wait("sync", "pe", pe_tile[t - 2])
            P.wait("scalar", "pe", pe_tile[t - 2])
            P.wait("gpsimd", "pe", pe_tile[t - 2])
        xtok = P.dma(xt[:, slot, :], xb_h[128 * t:128 * (t + 1), :])
        P.wait("scalar", "dS", xtok)
        P.op("scalar", lambda E, s=slot, tt=t: E.activation(
            xh_t(s), xt[:, s, :], AF.Identity, accum_out=ssum[:, tt:tt + 1]), seq=False)
        P.op("scalar", lambda E, s=slot, tt=t: E.activation(
            xl_t(s), xt[:, s, :], AF.Square, accum_out=ssq[:, tt:tt + 1]), seq=False)
        sct = P.tok("scalar")
        P.wait("gpsimd", "sc", sct)
        P.op("gpsimd", lambda E, s=slot: E.tensor_sub(
            xl_t(s), xt[:, s, :], xh_t(s)), seq=False)
        gpt = P.tok("gpsimd")
        P.wait("vector", "sc", sct)
        P.op("vector", lambda E, tt=t: E.tensor_scalar(
            mu_all[:, tt:tt + 1], ssum[:, tt:tt + 1], 1.0 / 1024, None, ALU.mult))
        P.op("vector", lambda E, tt=t: E.tensor_scalar(
            sd_t[:], ssq[:, tt:tt + 1], 1.0 / 1024, EPS, ALU.mult, ALU.add))
        P.op("vector", lambda E, tt=t: E.tensor_tensor(
            tiny[:, 0:1], mu_all[:, tt:tt + 1], mu_all[:, tt:tt + 1], ALU.mult))
        P.op("vector", lambda E: E.tensor_sub(sd_t[:], sd_t[:], tiny[:, 0:1]))
        v0 = P.tok("vector")
        P.wait("scalar", "ve", v0)
        P.op("scalar", lambda E: E.activation(sd_t[:], sd_t[:], AF.Sqrt), seq=False)
        sct2 = P.tok("scalar")
        P.wait("vector", "sc", sct2)
        P.op("vector", lambda E, tt=t: E.reciprocal(r_all[:, tt:tt + 1], sd_t[:]))
        ch = t // 2
        if t >= 1:
            P.wait("vector", "pe", pe_tile[t - 1])
            P.wait("gpsimd", "pe", pe_tile[t - 1])
        if t == 0:
            P.op("vector", lambda E: E.memset(pm32[:], 0.0))
        elif t % 2 == 0:
            P.op("vector", lambda E, c=ch - 1: E.memset(pm32[:, c:c + 1], 0.0))
        P.op("vector", lambda E, tt=t, c=ch: E.tensor_scalar(
            pm32[:, c:c + 1], r_all[:, tt:tt + 1], 1.0 / 256, None, ALU.mult))
        P.op("vector", lambda E: E.tensor_copy(pm_h[:], pm32[:]))
        P.op("vector", lambda E, tt=t: E.tensor_copy(mu_h[:], mu_all[:, tt:tt + 1]))
        vt = P.tok("vector")
        P.wait("gpsimd", "ve", vt)
        P.op("gpsimd", lambda E: E.tensor_sub(pm_l[:], pm32[:], pm_h[:]), seq=False)
        P.op("gpsimd", lambda E, tt=t: E.tensor_sub(mu_l[:], mu_all[:, tt:tt + 1], mu_h[:]),
             seq=False)
        gpt2 = P.tok("gpsimd")
        P.wait("tensor", "ve", vt)
        P.wait("tensor", "gp", gpt2)
        last = (t == NT - 1)
        mm3(0, ps[0][0:NLS, :], pm_h[:], pm_l[:],
            xh_t(slot)[:, 0:512], xl_t(slot)[:, 0:512], last=last)
        mm3(1, ps[1][0:NLS, :], pm_h[:], pm_l[:],
            xh_t(slot)[:, 512:1024], xl_t(slot)[:, 512:1024], last=last)
        mm3(2, ps[2][0:NLS, 0:1], pm_h[:], pm_l[:], mu_h[:], mu_l[:], last=last)
        pe_tile[t] = P.tok("tensor")
    P.barrier()

    pooled = final[0:NLS, 0, :]
    P.op("vector", lambda E: E.tensor_copy(corr[:], ps[2][0:NLS, 0:1]))
    P.op("vector", lambda E: E.tensor_scalar(
        pooled[:, 0:512], ps[0][0:NLS, :], corr[:], None, ALU.subtract))
    P.op("vector", lambda E: E.tensor_scalar(
        pooled[:, 512:1024], ps[1][0:NLS, :], corr[:], None, ALU.subtract))
    ptok = P.tok("vector")
    for i in range(3):
        bk.renew(i)
    P.wait("tensor", "ve", ptok)
    for f in range(8):
        bi = 3 + (f % 2)
        pe_take(bi)
        P.op("tensor", lambda E, ff=f, b=bi: E.transpose(
            ps[b][:, 0:NLS], pooled[:, 128 * ff:128 * (ff + 1)], idn[0:NLS, 0:NLS]))
        ttok = P.tok("tensor")
        P.wait("scalar", "pe", ttok)
        _, _, sct = dual_evac(ps[bi][:, 0:NLS], pfm_h[:, f, :], pfm_l[:, f, :],
                              stg_slot=f % 2)
        bk.renew(bi, ("sc", sct))
    P.barrier()
    for f in range(8):
        P.op("vector", lambda E, ff=f: E.tensor_reduce(
            tiny[:, 0:1], pfm_h[:, ff, :], axis=mybir.AxisListType.X, op=ALU.add))
        P.op("vector", lambda E, ff=f: E.tensor_reduce(
            tiny[:, 1:2], pfm_l[:, ff, :], axis=mybir.AxisListType.X, op=ALU.add))
        P.op("vector", lambda E: E.tensor_add(tiny[:, 0:1], tiny[:, 0:1], tiny[:, 1:2]))
        P.op("vector", lambda E, ff=f: E.tensor_scalar(
            gc32[:, ff:ff + 1], tiny[:, 0:1], 1.0 / 64, None, ALU.mult))
    P.op("vector", lambda E: E.tensor_copy(gc_h[:], gc32[:]))
    vt = P.tok("vector")
    P.wait("gpsimd", "ve", vt)
    P.op("gpsimd", lambda E: E.tensor_sub(gc_l[:], gc32[:], gc_h[:]))
    P.tok("gpsimd")
    P.barrier()

    wslot = {"i": 0, "tok": {}}

    def wstream(src_hi, src_lo=None, C=512):
        sl = wslot["i"] % 2
        wslot["i"] += 1
        if sl in wslot["tok"]:
            P.wait("sync", "pe", wslot["tok"][sl])
        if src_lo is None:
            vt_, gt_ = split_chunk(src_hi, 128, C, sl, wbh[:, sl, 0:C],
                                   wbl[:, sl, 0:C], False)
            P.wait("tensor", "ve", vt_)
            P.wait("tensor", "gp", gt_)
            return sl, None
        t1 = P.dma(wbh[:, sl, 0:C], src_hi)
        t2 = P.dma(wbl[:, sl, 0:C], src_lo)
        P.wait("tensor", "dS", [t1, t2])
        return sl, [t1, t2]

    def wmark(sl):
        wslot["tok"][sl] = P.tok("tensor")

    for f in range(8):
        sl, _ = wstream(win["wgk"][128 * f:128 * (f + 1), :], None, 512)
        for o in range(4):
            mm3(5, ps[5][:, 66 * o:66 * o + NLS],
                wbh[:, sl, 128 * o:128 * (o + 1)], wbl[:, sl, 128 * o:128 * (o + 1)],
                pfm_h[:, f, :], pfm_l[:, f, :], last=(f == 7))
        wmark(sl)
    tk = P.val("pe")
    P.wait("scalar", "pe", tk)
    dual_evac(ps[5][:, 0:264].rearrange("p (o c) -> p o c", c=66)[:, :, 0:NLS],
              gk_h[:], gk_l[:], stg_slot=0,
              stage=stg[:, 0, 0:264].rearrange("p (o c) -> p o c", c=66)[:, :, 0:NLS])
    bk.renew(5)
    for f in range(8):
        sl, _ = wstream(win["wgv"][128 * f:128 * (f + 1), :], None, 512)
        mm3(6, ps[6][0:NLS, :], pfm_h[:, f, :], pfm_l[:, f, :],
            wbh[:, sl, 0:512], wbl[:, sl, 0:512], last=(f == 7))
        wmark(sl)
    tk = P.val("pe")
    P.wait("scalar", "pe", tk)
    dual_evac(ps[6][0:NLS, 0:512].rearrange("p (h c) -> p h c", c=64),
              gv_h[:, :, 0:64], gv_l[:, :, 0:64], stg_slot=1,
              stage=stg[0:NLS, 1, 0:512].rearrange("p (h c) -> p h c", c=64))
    bk.renew(6)
    P.barrier()
    for f in range(8):
        sl, _ = wstream(win["gc_w1"][128 * f:128 * (f + 1), :], None, 512)
        for o in range(4):
            mm3(7, ps[7][:, 2 * o:2 * o + 1],
                wbh[:, sl, 128 * o:128 * (o + 1)], wbl[:, sl, 128 * o:128 * (o + 1)],
                gc_h[:, f:f + 1], gc_l[:, f:f + 1], last=(f == 7))
        wmark(sl)
    P.wait("scalar", "pe")
    for o in range(4):
        P.op("scalar", lambda E, oo=o: E.activation(
            h1gc[:, oo:oo + 1], ps[7][:, 2 * oo:2 * oo + 1], AF.Gelu,
            bias=gcb1[:, oo:oo + 1]))
    sct = P.tok("scalar")
    bk.renew(7)
    P.wait("vector", "sc", sct)
    P.op("vector", lambda E: E.tensor_copy(h1gc_h[:], h1gc[:]))
    vt = P.tok("vector")
    P.wait("gpsimd", "ve", vt)
    P.op("gpsimd", lambda E: E.tensor_sub(h1gc_l[:], h1gc[:], h1gc_h[:]))
    gt2 = P.tok("gpsimd")
    P.wait("tensor", "ve", vt)
    P.wait("tensor", "gp", gt2)
    P.wait("tensor", "sc", sct)
    for f in range(4):
        sl, _ = wstream(win["gc_w2"][128 * f:128 * (f + 1), :], None, 256)
        for o in range(2):
            mm3(7, ps[7][:, 2 * o:2 * o + 1],
                wbh[:, sl, 128 * o:128 * (o + 1)], wbl[:, sl, 128 * o:128 * (o + 1)],
                h1gc_h[:, f:f + 1], h1gc_l[:, f:f + 1], last=(f == 3))
        wmark(sl)
    P.wait("scalar", "pe")
    for o in range(2):
        P.op("scalar", lambda E, oo=o: E.activation(
            h2gc[:, oo:oo + 1], ps[7][:, 2 * oo:2 * oo + 1], AF.Gelu,
            bias=gcb2[:, oo:oo + 1]))
    sct = P.tok("scalar")
    bk.renew(7)
    P.wait("vector", "sc", sct)
    P.op("vector", lambda E: E.tensor_copy(h2gc_h[:], h2gc[:]))
    vt = P.tok("vector")
    P.wait("gpsimd", "ve", vt)
    P.op("gpsimd", lambda E: E.tensor_sub(h2gc_l[:], h2gc[:], h2gc_h[:]))
    gt2 = P.tok("gpsimd")
    P.wait("tensor", "ve", vt)
    P.wait("tensor", "gp", gt2)
    P.wait("tensor", "sc", sct)
    for f in range(2):
        sl, _ = wstream(win["gc_w3"][128 * f:128 * (f + 1), :], None, 1)
        mm3(7, ps[7][0:1, 0:1], wbh[:, sl, 0:1], wbl[:, sl, 0:1],
            h2gc_h[:, f:f + 1], h2gc_l[:, f:f + 1], last=(f == 1))
        wmark(sl)
    P.wait("scalar", "pe")
    P.op("scalar", lambda E: E.activation(gf32[:], ps[7][0:1, 0:1], AF.Sigmoid,
                                          bias=gcb3[:]))
    sct = P.tok("scalar")
    bk.renew(7)
    P.wait("vector", "sc", sct)
    P.op("vector", lambda E: E.tensor_scalar(
        tiny[0:1, 0:1], gf32[:], 0.7, -0.5, ALU.mult, ALU.add))
    P.op("vector", lambda E: E.tensor_tensor(tiny[0:1, 0:1], tiny[0:1, 0:1],
                                             ar_s[:], ALU.mult))
    P.op("vector", lambda E: E.tensor_scalar(
        ab32[0:1, 0:1], tiny[0:1, 0:1], 2.0, 0.1, ALU.mult, ALU.add))
    P.op("vector", lambda E: E.tensor_scalar(
        ab32[0:1, 1:2], ar_s[:], 0.6, None, ALU.mult))
    P.tok("vector")
    P.barrier()

    def flat(ap):
        return ap.rearrange("p t c -> p (t c)")

    xslot_tok = {}
    xnslot_tok = {}

    def proj_tiles(perm_tiles, is_group, g):
        ntl = len(perm_tiles)
        kvpos = [pt + 2 if pt < 32 else pt - 32 for pt in perm_tiles]
        evk = {}
        for ii, pt in enumerate(perm_tiles):
            slot = ii % 2
            if slot in xslot_tok:
                P.wait("sync", "ve", xslot_tok[slot])
            xtok = P.dma(xt[:, slot, :], xb_h[128 * pt:128 * (pt + 1), :])
            P.wait("vector", "dS", xtok)
            if slot in xnslot_tok:
                P.wait("vector", "pe", xnslot_tok[slot])
            P.op("vector", lambda E, s=slot, t=pt: E.tensor_scalar(
                xnt(s), xt[:, s, :], mu_all[:, t:t + 1], r_all[:, t:t + 1],
                ALU.subtract, ALU.mult))
            vt = P.tok("vector")
            xslot_tok[slot] = vt
            P.wait("tensor", "ve", vt)
            for half in range(2):
                b = half
                pe_take(b)
                for fi in range(4):
                    f = 4 * half + fi
                    P.op("tensor", lambda E, s=slot, ff=f, bb=b, fj=fi: E.transpose(
                        ps[bb][:, 128 * fj:128 * (fj + 1)],
                        xnt(s)[:, 128 * ff:128 * (ff + 1)], idn[:]))
                ttok = P.tok("tensor")
                if half == 1:
                    xnslot_tok[slot] = ttok
                P.wait("scalar", "pe", ttok)
                bv = ps[b][:, 0:512].rearrange("p (f c) -> p f c", c=128)
                if is_group:
                    stage = outr[:, 4 * half:4 * half + 4, ii, :]
                    slot_arg = None
                else:
                    stage = stg[:, half, 0:512].rearrange("p (f c) -> p f c", c=128)
                    slot_arg = half
                ve_t, gp_t, sct = dual_evac(
                    bv, xnf_h[:, 4 * half:4 * half + 4, ii, :],
                    xnf_l[:, 4 * half:4 * half + 4, ii, :], stg_slot=slot_arg,
                    stage=stage)
                if is_group:
                    evk.setdefault(half, []).append((ve_t, gp_t))
                else:
                    P.wait("tensor", "ve", ve_t)
                    P.wait("tensor", "gp", gp_t)
                bk.renew(b, ("sc", sct))
        ntok = 128 * ntl

        for f in range(8):
            sl, _ = wstream(scr["wk"][0, 128 * f:128 * (f + 1), :],
                            scr["wk"][1, 128 * f:128 * (f + 1), :], 1024)
            if f % 4 == 0 and (f // 4) in evk:
                for ve_t, gp_t in evk.pop(f // 4):
                    P.wait("tensor", "ve", ve_t)
                    P.wait("tensor", "gp", gp_t)
            for o in range(8):
                mm3(o, ps[o][:, 0:ntok],
                    wbh[:, sl, 128 * o:128 * (o + 1)], wbl[:, sl, 128 * o:128 * (o + 1)],
                    flat(xnf_h[:, f, 0:ntl, :]), flat(xnf_l[:, f, 0:ntl, :]),
                    last=(f == 7))
            wmark(sl)
        P.wait("scalar", "pe")
        kv_done = []
        for o in range(8):
            for ii, j in enumerate(kvpos):
                sl6 = j % KVR
                _, _, sct = dual_evac(ps[o][:, 128 * ii:128 * (ii + 1)],
                                      kf_h[:, o, sl6, :], kf_l[:, o, sl6, :],
                                      stg_slot=o % 2)
            bk.renew(o, ("sc", sct))
        for f in range(8):
            sl, _ = wstream(scr["wv"][0, 128 * f:128 * (f + 1), :],
                            scr["wv"][1, 128 * f:128 * (f + 1), :], 1024)
            for ii in range(ntl):
                for oh in range(2):
                    bi = 2 * ii + oh
                    mm3(bi, ps[bi][:, 0:512],
                        xnf_h[:, f, ii, :], xnf_l[:, f, ii, :],
                        wbh[:, sl, 512 * oh:512 * (oh + 1)],
                        wbl[:, sl, 512 * oh:512 * (oh + 1)], last=(f == 7))
            wmark(sl)
        P.wait("scalar", "pe")
        for ii, j in enumerate(kvpos):
            sl6 = j % KVR
            for oh in range(2):
                bi = 2 * ii + oh
                _, _, sct = dual_evac(
                    ps[bi][:, 0:512].rearrange("p (h c) -> p h c", c=64),
                    vt_h[:, sl6, 8 * oh:8 * oh + 8, 0:64],
                    vt_l[:, sl6, 8 * oh:8 * oh + 8, 0:64], stg_slot=oh,
                    stage=stg[:, oh, 0:512].rearrange("p (h c) -> p h c", c=64))
                bk.renew(bi, ("sc", sct))
        if not is_group:
            return

        for f in range(8):
            sl, _ = wstream(scr["wq"][0, 128 * f:128 * (f + 1), :],
                            scr["wq"][1, 128 * f:128 * (f + 1), :], 1024)
            for o in range(8):
                mm3(o, ps[o][:, 0:ntok],
                    wbh[:, sl, 128 * o:128 * (o + 1)], wbl[:, sl, 128 * o:128 * (o + 1)],
                    flat(xnf_h[:, f, 0:ntl, :]), flat(xnf_l[:, f, 0:ntl, :]),
                    last=(f == 7))
            wmark(sl)
        P.wait("scalar", "pe")
        q_ve = q_gp = None
        for o in range(8):
            q_ve, q_gp, sct = dual_evac(ps[o][:, 0:ntok], flat(qf_h[:, o, 0:ntl, :]),
                                        flat(qf_l[:, o, 0:ntl, :]), stg_slot=o % 2)
            bk.renew(o, ("sc", sct))
        P.wait("tensor", "ve", q_ve)
        P.wait("tensor", "gp", q_gp)
        for f in range(8):
            mm3(0, ps[0][0:32, 0:ntok], gw_h[:, f, :], gw_l[:, f, :],
                flat(qf_h[:, f, 0:ntl, :]), flat(qf_l[:, f, 0:ntl, :]), last=(f == 7))
        gmtk = P.tok("tensor")
        P.wait("scalar", "pe", gmtk)
        if 1 in stage_busy:
            P.wait("scalar", "gp", stage_busy[1])
        P.op("scalar", lambda E, n=ntok: E.copy(gstg[:, 0:n], ps[0][0:32, 0:n]))
        sct = P.tok("scalar")
        bk.renew(0, ("sc", sct))
        P.wait("tensor", "sc", sct)
        for ii in range(ntl):
            P.op("tensor", lambda E, i2=ii: E.transpose(
                ps[1][:, 32 * i2:32 * (i2 + 1)], gstg[0:32, 128 * i2:128 * (i2 + 1)],
                idn[0:32, 0:32]))
        tk2 = P.tok("tensor")
        P.wait("scalar", "pe", tk2)
        P.op("scalar", lambda E, n=ntl: E.activation(
            flat(g_tm[:, 0:n, :]), ps[1][:, 0:32 * n], AF.Sigmoid))
        sct = P.tok("scalar")
        bk.renew(1, ("sc", sct))
        for f in range(8):
            sl, _ = wstream(scr["wgq"][0, 128 * f:128 * (f + 1), :],
                            scr["wgq"][1, 128 * f:128 * (f + 1), :], 512)
            for o in range(4):
                mm3(2 + o, ps[2 + o][:, 0:ntok],
                    wbh[:, sl, 128 * o:128 * (o + 1)], wbl[:, sl, 128 * o:128 * (o + 1)],
                    flat(xnf_h[:, f, 0:ntl, :]), flat(xnf_l[:, f, 0:ntl, :]),
                    last=(f == 7))
            wmark(sl)
        P.wait("scalar", "pe")
        for o in range(4):
            _, _, sct = dual_evac(ps[2 + o][:, 0:ntok], flat(gqf_h[:, o, 0:ntl, :]),
                                  flat(gqf_l[:, o, 0:ntl, :]), stg_slot=o % 2)
            bk.renew(2 + o, ("sc", sct))

    def attention(g):
        T0 = GT * g
        mtok = P.dma(mlat_s[:], mlat_h[:, 512 * g:512 * (g + 1)])
        for h8 in range(LH):
            bsc = h8 % 2
            ch, ro = h8 // 2, 64 * (h8 % 2)
            mm3(bsc, ps[bsc][0:NLS, 0:512],
                gk_h[ro:ro + 64, ch, :], gk_l[ro:ro + 64, ch, :],
                flat(gqf_h[ro:ro + 64, ch, :, :]), flat(gqf_l[ro:ro + 64, ch, :, :]),
                last=True)
            tk = P.tok("tensor")
            P.wait("scalar", "pe", tk)
            P.op("scalar", lambda E, b=bsc: E.activation(
                plat32, ps[b][0:NLS, 0:512], AF.Exp, scale=0.125), seq=False)
            sct = P.tok("scalar")
            bk.renew(bsc, ("sc", sct))
            P.wait("gpsimd", "sc", sct)
            P.wait("gpsimd", "dS", mtok)
            P.op("gpsimd", lambda E: E.tensor_tensor(plat32, plat32, mlat_s[:],
                                                     ALU.mult), seq=False)
            g1 = P.tok("gpsimd")
            P.wait("vector", "gp", g1)
            P.op("vector", lambda E: E.tensor_copy(plat_h[:], plat32), seq=False)
            v1 = P.tok("vector")
            P.wait("gpsimd", "ve", v1)
            P.op("gpsimd", lambda E: E.tensor_sub(plat_l[:], plat32, plat_h[:]), seq=False)
            g2 = P.tok("gpsimd")
            P.wait("tensor", "ve", v1)
            P.wait("tensor", "gp", g2)
            for ti in range(GT):
                bpv = 2 + (ti % 2)
                mm3(bpv, ps[bpv][:, 0:65],
                    plat_h[:, 128 * ti:128 * (ti + 1)], plat_l[:, 128 * ti:128 * (ti + 1)],
                    gv_h[:, h8, :], gv_l[:, h8, :], last=True)
                tk = P.tok("tensor")
                P.wait("vector", "pe", tk)
                P.op("vector", lambda E, b=bpv: E.reciprocal(tiny[:, 0:1],
                                                             ps[b][:, 64:65]), seq=False)
                P.op("vector", lambda E, b=bpv, t=ti, h=h8: E.tensor_scalar(
                    rem_tm[:, t, 64 * h:64 * (h + 1)], ps[b][:, 0:64], tiny[:, 0:1],
                    None, ALU.mult))
                vtk = P.tok("vector")
                bk.renew(bpv, ("ve", vtk))
        for ti in range(GT):
            pe_take(4)
            for fi in range(4):
                P.op("tensor", lambda E, t=ti, f=fi: E.transpose(
                    ps[4][:, 128 * f:128 * (f + 1)], rem_tm[:, t, 128 * f:128 * (f + 1)],
                    idn[:]))
            tk = P.tok("tensor")
            P.wait("scalar", "pe", tk)
            ve_t, gp_t, sct4 = dual_evac(
                ps[4][:, 0:512].rearrange("p (f c) -> p f c", c=128),
                remf_h[:, :, ti, :], remf_l[:, :, ti, :], stg_slot=ti % 2,
                stage=stg[:, ti % 2, 0:512].rearrange("p (f c) -> p f c", c=128))
            P.wait("tensor", "ve", ve_t)
            P.wait("tensor", "gp", gp_t)
            bk.renew(4, ("sc", sct4))
        P.barrier()

        for sw in range(2):
            for f in range(4):
                sl = wslot["i"] % 2
                wslot["i"] += 1
                if sl in wslot["tok"]:
                    P.wait("sync", "pe", wslot["tok"][sl])
                t1 = P.dma(wbh[:, sl, 0:512],
                           scr["r_out"][0, 128 * f:128 * (f + 1), 512 * sw:512 * (sw + 1)])
                t2 = P.dma(wbl[:, sl, 0:512],
                           scr["r_out"][1, 128 * f:128 * (f + 1), 512 * sw:512 * (sw + 1)])
                P.wait("tensor", "dS", [t1, t2])
                for ti in range(GT):
                    mm3(ti, ps[ti][:, 0:512],
                        remf_h[:, f, ti, :], remf_l[:, f, ti, :],
                        wbh[:, sl, 0:512], wbl[:, sl, 0:512], last=(f == 3))
                wmark(sl)
            P.wait("vector", "pe")
            for ti in range(GT):
                for hh in range(8):
                    ha = 8 * sw + hh
                    P.op("vector", lambda E, t=ti, h=hh, a=ha: E.tensor_scalar(
                        ghalf[:, t, 64 * h:64 * (h + 1)], ps[t][:, 64 * h:64 * (h + 1)],
                        g_tm[:, t, 16 + a:17 + a], None, ALU.mult), seq=False)
            vtk = P.tok("vector")
            for ti in range(GT):
                bk.renew(ti, ("ve", vtk))

            comp_rel = {}

            def pv_reg(T, hh):
                base = (0, 2, 4, 0)[T - T0]
                return base + hh // 4, 66 * (hh % 4)

            stages = []
            for kj in range(6):
                kvj = 4 * g + kj
                Tlo = max(T0, kvj - 2)
                Thi = min(T0 + GT - 1, kvj)
                for hh, ha in enumerate(range(8 * sw, 8 * sw + 8)):
                    stages.append((kj, kvj, Tlo, Thi, hh, ha))

            def issue_front(si):
                kj, kvj, Tlo, Thi, hh, ha = stages[si]
                w = Thi - Tlo + 1
                ch, ro = ha // 2, 64 * (ha % 2)
                bsc = 6 + (si % 2)
                sl6 = kvj % KVR
                pb = si % 2
                pp = p32bufs[pb]
                mm3(bsc, ps[bsc][:, 0:128 * w],
                    kf_h[ro:ro + 64, ch, sl6, :], kf_l[ro:ro + 64, ch, sl6, :],
                    flat(qf_h[ro:ro + 64, ch, Tlo - T0:Thi - T0 + 1, :]),
                    flat(qf_l[ro:ro + 64, ch, Tlo - T0:Thi - T0 + 1, :]), last=True)
                tk = P.tok("tensor")
                P.wait("scalar", "pe", tk)
                if p32_rd[pb] is not None:
                    P.wait("scalar", "ve", p32_rd[pb])
                P.op("scalar", lambda E, b=bsc, ww=w, pp=pp: E.activation(
                    pp[:, 0:128 * ww], ps[b][:, 0:128 * ww], AF.Exp, scale=0.125),
                    seq=False)
                sct = P.tok("scalar")
                bk.renew(bsc, ("sc", sct))
                P.wait("gpsimd", "sc", sct)
                nm = 0
                for bi_, T in enumerate(range(Tlo, Thi + 1)):
                    off = T - (kvj - 2)
                    if off == 0:
                        P.op("gpsimd", lambda E, b=bi_, pp=pp: E.tensor_tensor(
                            pp[:, 128 * b:128 * (b + 1)],
                            pp[:, 128 * b:128 * (b + 1)], mtri[:, 0, :], ALU.mult),
                            seq=False)
                        nm += 1
                    elif off == 2:
                        P.op("gpsimd", lambda E, b=bi_, pp=pp: E.tensor_tensor(
                            pp[:, 128 * b:128 * (b + 1)],
                            pp[:, 128 * b:128 * (b + 1)], mtri[:, 1, :], ALU.mult),
                            seq=False)
                        nm += 1
                if kvj < 2:
                    if nm:
                        gmt = P.tok("gpsimd")
                        P.wait("gpsimd", "gp", gmt)
                    P.op("gpsimd", lambda E, j=kvj, ww=w, pp=pp: E.tensor_scalar(
                        pp[:, 0:128 * ww], pp[:, 0:128 * ww],
                        medge[:, j:j + 1], None, ALU.mult), seq=False)
                g1 = P.tok("gpsimd")
                P.wait("vector", "gp", g1)
                P.op("vector", lambda E, ww=w, b=pb, pp=pp: E.tensor_copy(
                    pl_h[:, b, 0:128 * ww], pp[:, 0:128 * ww]), seq=False)
                P.op("vector", lambda E, ww=w, b=pb, pp=pp: E.tensor_sub(
                    pl_l[:, b, 0:128 * ww], pp[:, 0:128 * ww], pl_h[:, b, 0:128 * ww]))
                v1 = P.tok("vector")
                p32_rd[pb] = v1
                return v1

            def issue_pv(si, v1):
                kj, kvj, Tlo, Thi, hh, ha = stages[si]
                sl6 = kvj % KVR
                pb = si % 2
                if kj == 3 and hh == 0:
                    bk.renew(0, comp_rel.get(T0))
                    bk.renew(1, comp_rel.get(T0))
                P.wait("tensor", "ve", v1)
                for bi_, T in enumerate(range(Tlo, Thi + 1)):
                    bb, cc = pv_reg(T, hh)
                    mm3(bb, ps[bb][:, cc:cc + 65],
                        pl_h[:, pb, 128 * bi_:128 * (bi_ + 1)],
                        pl_l[:, pb, 128 * bi_:128 * (bi_ + 1)],
                        vt_h[:, sl6, ha, :], vt_l[:, sl6, ha, :],
                        last=(kvj == T + 2))
                if hh == 7:
                    Tc = kvj - 2
                    if Tc >= T0:
                        tk = P.tok("tensor")
                        P.wait("vector", "pe", tk)
                        tl = Tc - T0
                        for h2, a2 in enumerate(range(8 * sw, 8 * sw + 8)):
                            bb, cc = pv_reg(Tc, h2)
                            P.op("vector", lambda E, b=bb, c=cc, h=h2: E.reciprocal(
                                tiny[:, h:h + 1], ps[b][:, c + 64:c + 65]), seq=False)
                            P.op("vector", lambda E, h=h2, t=tl, a=a2: E.tensor_tensor(
                                tiny[:, h:h + 1], tiny[:, h:h + 1],
                                g_tm[:, t, a:a + 1], ALU.mult))
                            P.op("vector", lambda E, b=bb, c=cc, h=h2, t=tl:
                                 E.scalar_tensor_tensor(
                                     ghalf[:, t, 64 * h:64 * (h + 1)], ps[b][:, c:c + 64],
                                     tiny[:, h:h + 1], ghalf[:, t, 64 * h:64 * (h + 1)],
                                     ALU.mult, ALU.add))
                        vtk = P.tok("vector")
                        comp_rel[Tc] = ("ve", vtk)

            prev_v1 = None
            for si in range(len(stages)):
                v1 = issue_front(si)
                if si > 0:
                    issue_pv(si - 1, prev_v1)
                prev_v1 = v1
            issue_pv(len(stages) - 1, prev_v1)
            for b in range(6):
                bk.renew(b, comp_rel.get(T0 + 3))
            for ti in range(GT):
                pe_take(6)
                for fi in range(4):
                    P.op("tensor", lambda E, t=ti, f=fi: E.transpose(
                        ps[6][:, 128 * f:128 * (f + 1)],
                        ghalf[:, t, 128 * f:128 * (f + 1)], idn[:]))
                tk = P.tok("tensor")
                P.wait("scalar", "pe", tk)
                ve_t, gp_t, sct6 = dual_evac(
                    ps[6][:, 0:512].rearrange("p (f c) -> p f c", c=128),
                    gfm_h[:, :, ti, :], gfm_l[:, :, ti, :], stg_slot=ti % 2,
                    stage=stg[:, ti % 2, 0:512].rearrange("p (f c) -> p f c", c=128))
                bk.renew(6, ("sc", sct6))
            P.wait("tensor", "ve", ve_t)
            P.wait("tensor", "gp", gp_t)
            for f in range(4):
                sl = wslot["i"] % 2
                wslot["i"] += 1
                if sl in wslot["tok"]:
                    P.wait("sync", "pe", wslot["tok"][sl])
                t1 = P.dma(wbh[:, sl, :],
                           scr["w_out"][0, 512 * sw + 128 * f:512 * sw + 128 * (f + 1), :])
                t2 = P.dma(wbl[:, sl, :],
                           scr["w_out"][1, 512 * sw + 128 * f:512 * sw + 128 * (f + 1), :])
                P.wait("tensor", "dS", [t1, t2])
                for o in range(8):
                    mm3(o, ps[o][:, 0:512],
                        wbh[:, sl, 128 * o:128 * (o + 1)], wbl[:, sl, 128 * o:128 * (o + 1)],
                        flat(gfm_h[:, f, :, :]), flat(gfm_l[:, f, :, :]), last=(f == 3))
                wmark(sl)
            for o in range(8):
                P.wait("scalar", "pe")
                if (o % 2) in stage_busy:
                    P.wait("scalar", "gp", stage_busy[o % 2])
                P.op("scalar", lambda E, b=o: E.copy(stg[:, b % 2, :], ps[b][:, 0:512]))
                sct = P.tok("scalar")
                P.wait("tensor", "sc", sct)
                for ti in range(GT):
                    P.op("tensor", lambda E, b=o, t=ti: E.transpose(
                        ps[b][:, 128 * t:128 * (t + 1)],
                        stg[:, b % 2, 128 * t:128 * (t + 1)], idn[:]))
                tk = P.tok("tensor")
                stage_busy[o % 2] = 0
                P.wait("vector", "pe", tk)
                dst = final[:, :, 128 * o:128 * (o + 1)]
                src6 = ps[o][:, 0:512].rearrange("p (t c) -> p t c", c=128)
                if sw == 0:
                    P.op("vector", lambda E, d=dst, s6=src6: E.tensor_copy(d, s6),
                         seq=False)
                else:
                    P.op("vector", lambda E, d=dst, s6=src6: E.tensor_add(d, d, s6),
                         seq=False)
                vtk = P.tok("vector")
                bk.renew(o, ("ve", vtk))
            stage_busy.clear()
        P.barrier()

    def ffn(g):
        split_rel = None
        for cyc in range(3):
            inh = xnf_h if cyc == 0 else outh
            inl = xnf_l if cyc == 0 else outl
            forder = list(range(8)) if split_rel is None else [0, 4, 1, 5, 2, 6, 3, 7]
            for fi_, f in enumerate(forder):
                sl, _ = wstream(scr["cn_w1"][0, 128 * f:128 * (f + 1), :],
                                scr["cn_w1"][1, 128 * f:128 * (f + 1), :], 1024)
                if split_rel is not None:
                    P.wait("tensor", "ve", split_rel[f][0])
                    P.wait("tensor", "gp", split_rel[f][1])
                for o in range(8):
                    mm3(o, ps[o][:, 0:512],
                        wbh[:, sl, 128 * o:128 * (o + 1)], wbl[:, sl, 128 * o:128 * (o + 1)],
                        flat(inh[:, f, :, :]), flat(inl[:, f, :, :]), last=(fi_ == 7))
                wmark(sl)
            P.wait("scalar", "pe")
            ev1 = []
            for o in range(8):
                h_ve, h_gp, sct = dual_evac(ps[o][:, 0:512], flat(qf_h[:, o, :, :]),
                                            flat(qf_l[:, o, :, :]), stg_slot=o % 2,
                                            func=AF.Gelu, bias=cnb1[:, o:o + 1])
                bk.renew(o, ("sc", sct))
                ev1.append((h_ve, h_gp))
            for f in range(8):
                sl, _ = wstream(scr["cn_w2"][0, 128 * f:128 * (f + 1), :],
                                scr["cn_w2"][1, 128 * f:128 * (f + 1), :], 512)
                P.wait("tensor", "ve", ev1[f][0])
                P.wait("tensor", "gp", ev1[f][1])
                for o in range(4):
                    mm3(o, ps[o][:, 0:512],
                        wbh[:, sl, 128 * o:128 * (o + 1)], wbl[:, sl, 128 * o:128 * (o + 1)],
                        flat(qf_h[:, f, :, :]), flat(qf_l[:, f, :, :]), last=(f == 7))
                wmark(sl)
            P.wait("scalar", "pe")
            ev2 = []
            for o in range(4):
                h_ve, h_gp, sct = dual_evac(ps[o][:, 0:512], flat(gqf_h[:, o, :, :]),
                                            flat(gqf_l[:, o, :, :]), stg_slot=o % 2,
                                            func=AF.Gelu, bias=cnb2[:, o:o + 1])
                bk.renew(o, ("sc", sct))
                ev2.append((h_ve, h_gp))
            for f in range(4):
                sl, _ = wstream(scr["cn_w3"][0, 128 * f:128 * (f + 1), :],
                                scr["cn_w3"][1, 128 * f:128 * (f + 1), :], 256)
                P.wait("tensor", "ve", ev2[f][0])
                P.wait("tensor", "gp", ev2[f][1])
                for o in range(2):
                    mm3(o, ps[o][:, 0:512],
                        wbh[:, sl, 128 * o:128 * (o + 1)], wbl[:, sl, 128 * o:128 * (o + 1)],
                        flat(gqf_h[:, f, :, :]), flat(gqf_l[:, f, :, :]), last=(f == 3))
                wmark(sl)
            P.wait("scalar", "pe")
            ev3 = []
            for o in range(2):
                h_ve, h_gp, sct = dual_evac(ps[o][:, 0:512], flat(xnf_h[:, o, :, :]),
                                            flat(xnf_l[:, o, :, :]), stg_slot=o % 2,
                                            func=AF.Gelu, bias=cnb3[:, o:o + 1])
                bk.renew(o, ("sc", sct))
                ev3.append((h_ve, h_gp))
            for f in range(2):
                P.wait("tensor", "ve", ev3[f][0])
                P.wait("tensor", "gp", ev3[f][1])
                mm3(2, ps[2][0:1, 0:512], cn4h[:, f, :], cn4l[:, f, :],
                    flat(xnf_h[:, f, :, :]), flat(xnf_l[:, f, :, :]), last=(f == 1))
            tk = P.tok("tensor")
            P.wait("scalar", "pe", tk)
            P.op("scalar", lambda E: E.activation(tf32, ps[2][0:1, 0:512],
                                                  AF.Sigmoid, bias=cnb4[:]))
            sct = P.tok("scalar")
            bk.renew(2, ("sc", sct))
            P.wait("vector", "sc", sct)
            P.op("vector", lambda E: E.tensor_scalar(
                c1row, tf32, ab32[0:1, 1:2], ab32[0:1, 0:1], ALU.mult, ALU.add))
            vtk = P.tok("vector")
            P.wait("tensor", "ve", vtk)
            mm(3, ps[3][:, 0:512], ones32[:], c1row, last=True)
            tk = P.tok("tensor")
            P.wait("scalar", "pe", tk)
            P.op("scalar", lambda E: E.copy(c32, ps[3][:, 0:512]))
            sct = P.tok("scalar")
            bk.renew(3, ("sc", sct))
            P.wait("vector", "sc", sct)
            P.wait("gpsimd", "sc", sct)
            pairs = [(0, ALU.add, ALU.subtract), (1, ALU.add, ALU.subtract),
                     (2, ALU.subtract, ALU.add), (3, ALU.subtract, ALU.add)]
            va_toks = []
            for pi, (fa, sa, sbn) in enumerate(pairs):
                fb = fa + 4
                eng = "vector" if pi < 3 else "gpsimd"
                ta = tsc0 if pi < 3 else stg[:, 0, :]
                tb = tsc1 if pi < 3 else stg[:, 1, :]
                P.op(eng, lambda E, a=fa, t=ta: E.tensor_tensor(
                    t, c32, flat(outr[:, a, :, :]), ALU.mult))
                P.op(eng, lambda E, b=fb, t=tb: E.tensor_tensor(
                    t, c32, flat(outr[:, b, :, :]), ALU.mult), seq=False)
                P.op(eng, lambda E, b=fb, t=ta, o=sbn: E.tensor_tensor(
                    flat(outr[:, b, :, :]), flat(outr[:, b, :, :]), t, o))
                P.op(eng, lambda E, a=fa, t=tb, o=sa: E.tensor_tensor(
                    flat(outr[:, a, :, :]), flat(outr[:, a, :, :]), t, o),
                    seq=False)
                if pi < 3 and cyc < 2:
                    P.op("vector", lambda E, f=fa: E.tensor_copy(
                        flat(outh[:, f, :, :]), flat(outr[:, f, :, :])))
                    P.op("vector", lambda E, f=fb: E.tensor_copy(
                        flat(outh[:, f, :, :]), flat(outr[:, f, :, :])), seq=False)
                    va_toks.append((fa, fb, P.tok("vector")))
            gpt_tw = P.tok("gpsimd")
            stage_busy[0] = gpt_tw
            stage_busy[1] = gpt_tw
            if cyc < 2:
                P.wait("vector", "gp", gpt_tw)
                P.op("vector", lambda E: E.tensor_copy(
                    flat(outh[:, 3, :, :]), flat(outr[:, 3, :, :])))
                P.op("vector", lambda E: E.tensor_copy(
                    flat(outh[:, 7, :, :]), flat(outr[:, 7, :, :])), seq=False)
                v2 = P.tok("vector")
                split_toks = {}
                for fa, fb, va in va_toks:
                    P.wait("gpsimd", "ve", va)
                    P.op("gpsimd", lambda E, f=fa: E.tensor_sub(
                        flat(outl[:, f, :, :]), flat(outr[:, f, :, :]),
                        flat(outh[:, f, :, :])))
                    P.op("gpsimd", lambda E, f=fb: E.tensor_sub(
                        flat(outl[:, f, :, :]), flat(outr[:, f, :, :]),
                        flat(outh[:, f, :, :])), seq=False)
                    gt = P.tok("gpsimd")
                    split_toks[fa] = (va, gt)
                    split_toks[fb] = (va, gt)
                P.wait("gpsimd", "ve", v2)
                for f in (3, 7):
                    P.op("gpsimd", lambda E, ff=f: E.tensor_sub(
                        flat(outl[:, ff, :, :]), flat(outr[:, ff, :, :]),
                        flat(outh[:, ff, :, :])), seq=(f == 3))
                gp2 = P.tok("gpsimd")
                split_toks[3] = (v2, gp2)
                split_toks[7] = (v2, gp2)
                split_rel = split_toks
        P.barrier()
        for ti in range(GT):
            for half in range(2):
                pe_take(half)
                for fi in range(4):
                    f = 4 * half + fi
                    P.op("tensor", lambda E, t=ti, ff=f, h=half, j=fi: E.transpose(
                        ps[h][:, 128 * j:128 * (j + 1)], outr[:, ff, t, :], idn[:]))
                tk = P.tok("tensor")
                P.wait("vector", "pe", tk)
                P.op("vector", lambda E, t=ti, h=half: E.tensor_add(
                    final[:, t, 512 * h:512 * (h + 1)],
                    final[:, t, 512 * h:512 * (h + 1)], ps[h][:, 0:512]))
                vtk = P.tok("vector")
                bk.renew(half, ("ve", vtk))
            pt = GT * g + ti
            slot = ti % 2
            if slot in xslot_tok:
                P.wait("sync", "ve", xslot_tok[slot])
            xtok = P.dma(xt[:, slot, :], xb_h[128 * pt:128 * (pt + 1), :])
            P.wait("vector", "dS", xtok)
            P.op("vector", lambda E, t=ti, s=slot: E.tensor_add(
                final[:, t, :], final[:, t, :], xt[:, s, :]))
            vtk = P.tok("vector")
            xslot_tok[slot] = vtk
            P.wait("sync", "ve", vtk)
            P.dma(y_h[128 * pt:128 * (pt + 1), :], final[:, ti, :], sem="st")

    if DEBUG_DUMPS:
        P.dma(dbg["d_stats"][:, 0:NT], ssum[:], sem="st")
        P.dma(dbg["d_stats"][:, NT:2 * NT], ssq[:], sem="st")
        P.dma(dbg["d_stats"][:, 2 * NT:3 * NT], mu_all[:], sem="st")
        P.dma(dbg["d_stats"][:, 3 * NT:4 * NT], r_all[:], sem="st")
        for nm, hi, lo, shp in [("d_pfm", pfm_h, pfm_l, None),
                                ("d_gk", gk_h, gk_l, None),
                                ("d_gv", gv_h, gv_l, None)]:
            t32 = final[0:hi.shape[0], 0, 0:1]
        P.op("vector", lambda E: E.tensor_add(
            outr[:, 0:4, :, :].rearrange("p a t c -> p (a t c)")[:, 0:8 * NLS]
            .rearrange("p (a b) -> p a b", b=NLS),
            pfm_h[:], pfm_l[:]))
        vtk = P.tok("vector")
        P.wait("sync", "ve", vtk)
        P.dma(dbg["d_pfm"][:], outr[:, 0:4, :, :].rearrange("p a t c -> p (a t c)")
              [:, 0:8 * NLS].rearrange("p (a b) -> p a b", b=NLS), sem="st")
        P.op("vector", lambda E: E.tensor_add(
            outr[:, 4:6, :, :].rearrange("p a t c -> p (a t c)")[:, 0:4 * NLS]
            .rearrange("p (a b) -> p a b", b=NLS), gk_h[:], gk_l[:]))
        vtk = P.tok("vector")
        P.wait("sync", "ve", vtk)
        P.dma(dbg["d_gk"][:], outr[:, 4:6, :, :].rearrange("p a t c -> p (a t c)")
              [:, 0:4 * NLS].rearrange("p (a b) -> p a b", b=NLS), sem="st")
        P.op("vector", lambda E: E.tensor_add(
            outr[0:NLS, 6:8, :, :].rearrange("p a t c -> p (a t c)")[:, 0:LH * NLS]
            .rearrange("p (a b) -> p a b", b=NLS), gv_h[:], gv_l[:]))
        vtk = P.tok("vector")
        P.wait("sync", "ve", vtk)
        P.dma(dbg["d_gv"][:], outr[0:NLS, 6:8, :, :].rearrange("p a t c -> p (a t c)")
              [:, 0:LH * NLS].rearrange("p (a b) -> p a b", b=NLS), sem="st")
        P.op("vector", lambda E: E.tensor_copy(arena[0:1, 0:1], gf32[:]))
        P.op("vector", lambda E: E.tensor_copy(arena[0:1, 1:3], ab32[:]))
        vtk = P.tok("vector")
        P.wait("sync", "ve", vtk)
        P.dma(dbg["d_misc"][:], arena[0:1, 0:8], sem="st")
        P.barrier()

    proj_tiles([32, 33], False, -1)
    P.barrier()
    for g in range(NG):
        proj_tiles(list(range(GT * g, GT * (g + 1))), True, g)
        P.barrier()
        if DEBUG_DUMPS and g == 0:
            P.op("vector", lambda E: E.tensor_add(
                outr[:].rearrange("p a t c -> p (a t c)"),
                xnf_h[:].rearrange("p a t c -> p (a t c)"),
                xnf_l[:].rearrange("p a t c -> p (a t c)")))
            vtk = P.tok("vector")
            P.wait("sync", "ve", vtk)
            P.dma(dbg["d_xnf"][:], outr[:], sem="st")
            P.wait("vector", "st", P.val("st"))
            P.op("vector", lambda E: E.tensor_add(
                outr[:].rearrange("p a t c -> p (a t c)"),
                qf_h[:].rearrange("p a t c -> p (a t c)"),
                qf_l[:].rearrange("p a t c -> p (a t c)")))
            vtk = P.tok("vector")
            P.wait("sync", "ve", vtk)
            P.dma(dbg["d_qf"][:], outr[:], sem="st")
            P.dma(dbg["d_gtm"][:], g_tm[:], sem="st")
            P.dma(dbg["d_kfh"][:], kf_h[:], sem="st")
            P.dma(dbg["d_kfl"][:], kf_l[:], sem="st")
            P.dma(dbg["d_vth"][:], vt_h[:], sem="st")
            P.dma(dbg["d_vtl"][:], vt_l[:], sem="st")
            P.barrier()
        attention(g)
        if DEBUG_DUMPS and g == 0:
            vtk = P.tok("vector")
            P.wait("sync", "ve", vtk)
            P.dma(dbg["d_ghalf"][:], ghalf[:], sem="st")
            P.barrier()
        ffn(g)
        P.barrier()
    P.wait("sync", "st", P.val("st"))
    P.finalize_chain()

    blk = es.enter_context(nc.Block())

    def make_section(ename):
        def section(E):
            for c in P.cmds[ename]:
                if c[0] == "drain":
                    E.drain()
                elif c[0] == "wait":
                    E.wait_ge(sems[c[1]], c[2])
                elif c[0] == "dma":
                    E.dma_start(out=c[1], in_=c[2]).then_inc(sems[c[3]], 16)
                else:
                    inst = c[1](E)
                    if c[2] is not None:
                        inst.then_inc(sems[c[2][0]], 1)
        return section

    blk.sync(make_section("sync"))
    blk.tensor(make_section("tensor"))
    blk.vector(make_section("vector"))
    blk.scalar(make_section("scalar"))
    blk.gpsimd(make_section("gpsimd"))
    es.close()
    return nc


def _host_inputs(inputs):
    x = np.ascontiguousarray(np.asarray(inputs["L0_attn_x"], dtype=np.float32))
    glw = np.asarray(inputs["L0_attn_gate_local_w"], dtype=np.float32)
    grw = np.asarray(inputs["L0_attn_gate_remote_w"], dtype=np.float32)
    gw = np.zeros((1024, 32), np.float32)
    for hh in range(16):
        gw[64 * hh:64 * (hh + 1), hh] = glw[hh]
        gw[64 * hh:64 * (hh + 1), 16 + hh] = grw[hh]
    mtri = np.zeros((128, 2, 128), np.float32)
    mtri[:, 0, :] = (np.arange(128)[:, None] <= np.arange(128)[None, :])
    mtri[:, 1, :] = (np.arange(128)[:, None] > np.arange(128)[None, :])
    idn = np.eye(128, dtype=np.float32)
    shared = {
        "wq": inputs["L0_attn_wq"], "wk": inputs["L0_attn_wk"], "wv": inputs["L0_attn_wv"],
        "wgq": inputs["L0_attn_wgq"], "wgk": inputs["L0_attn_wgk"],
        "wgv": inputs["L0_attn_wgv"],
        "r_out": inputs["L0_attn_r_out_w"], "w_out": inputs["L0_attn_w_out"],
        "cn_w1": inputs["L0_ffn_cn_w1"], "cn_w2": inputs["L0_ffn_cn_w2"],
        "cn_w3": inputs["L0_ffn_cn_w3"], "cn_w4": inputs["L0_ffn_cn_w4"],
        "gc_w1": inputs["L0_ffn_gc_w1"], "gc_w2": inputs["L0_ffn_gc_w2"],
        "gc_w3": inputs["L0_ffn_gc_w3"], "gw": gw,
        "cn_b1": np.asarray(inputs["L0_ffn_cn_b1"]).reshape(8, 128).T,
        "cn_b2": np.asarray(inputs["L0_ffn_cn_b2"]).reshape(4, 128).T,
        "cn_b3": np.asarray(inputs["L0_ffn_cn_b3"]).reshape(2, 128).T,
        "cn_b4": np.asarray(inputs["L0_ffn_cn_b4"]).reshape(1, 1),
        "gc_b1": np.asarray(inputs["L0_ffn_gc_b1"]).reshape(4, 128).T,
        "gc_b2": np.asarray(inputs["L0_ffn_gc_b2"]).reshape(2, 128).T,
        "gc_b3": np.asarray(inputs["L0_ffn_gc_b3"]).reshape(1, 1),
        "ar": np.asarray(inputs["L0_ffn_adaptive_range"]).reshape(1, 1),
        "mtri": mtri, "idn": idn,
    }
    shared = {k: np.ascontiguousarray(np.asarray(v, dtype=np.float32))
              for k, v in shared.items()}
    in_maps = []
    for core in range(8):
        b, q = core // 4, core % 4
        xb_full = x[b]
        own_orig = list(range(16 * q, 16 * q + 16))
        halo_orig = 16 * q - 1 if q > 0 else None
        rest_orig = [i for i in range(64) if i not in own_orig and i != halo_orig]
        chunks = [xb_full[256 * i:256 * (i + 1)] for i in own_orig]
        chunks.append(xb_full[4096 * q - 256:4096 * q] if q > 0
                      else np.zeros((256, 1024), np.float32))
        chunks += [xb_full[256 * i:256 * (i + 1)] for i in rest_orig]
        origs = own_orig + [halo_orig] + rest_orig
        while len(origs) < NLS:
            chunks.append(np.zeros((256, 1024), np.float32))
            origs.append(None)
        xb_perm = np.ascontiguousarray(np.concatenate(chunks, 0))
        medge = (np.ones((128, 2), np.float32) if q > 0
                 else np.zeros((128, 2), np.float32))
        mlat = np.zeros((NLS, 4096), np.float32)
        tpos = (4096 * q + np.arange(4096)) // 256
        for j, o in enumerate(origs):
            if o is not None:
                mlat[j] = (o <= tpos)
        m = dict(shared)
        m["xb"] = xb_perm
        m["medge"] = np.ascontiguousarray(medge)
        m["mlat"] = np.ascontiguousarray(mlat)
        in_maps.append(m)
    return in_maps


def kernel(**inputs):
    in_maps = _host_inputs(inputs)
    nc = build_program()
    res = run_bass_kernel_spmd(nc, in_maps, core_ids=list(range(8)))
    out = np.zeros((2, 16384, 1024), np.float32)
    for core in range(8):
        b, q = core // 4, core % 4
        out[b, 4096 * q:4096 * (q + 1)] = res.results[core]["y"]
    return out
